# version 10
# baseline (speedup 1.0000x reference)
import contextlib
import numpy as np
import ml_dtypes
import concourse.bass as bass
import concourse.mybir as mybir
from concourse.bass_utils import run_bass_kernel_spmd

F32 = mybir.dt.float32
BF16 = mybir.dt.bfloat16
AF = mybir.ActivationFunctionType
ALU = mybir.AluOpType
AX = mybir.AxisListType

D = 1024
T = 4096
S = 1024
NSUP = T // S
CCN = 8
FF = 2048
NFC = FF // 128
NE = 8
NMC = 8
LTAP = 64
WCOL = 1088
Q = 128
NQ = S // Q
EPS = 1e-5
NTILE = T // 128
TAPG = 4
OVERLAP_RZ = False
_DBG = set()


class Emitter:
    QS = ("pe", "ve", "sc", "gp", "sy")

    def __init__(self):
        self.q = {k: [] for k in self.QS}
        self.cnt = {k: 0 for k in ("pe", "ve", "sc", "gd", "od")}
        self.wm = {}
        self.sems = {}

    def tok(self, sem):
        return (sem, self.cnt[sem])

    def wait(self, qn, token):
        if token is None:
            return
        sem, val = token
        if val <= 0:
            return
        key = (qn, sem)
        if self.wm.get(key, 0) >= val:
            return
        self.wm[key] = val
        h = self.sems[sem]
        self.q[qn].append(lambda eng, h=h, v=val: eng.wait_ge(h, v))

    def waits(self, qn, tokens):
        for t in tokens:
            self.wait(qn, t)

    def op(self, qn, fn, inc=None, amt=1):
        if inc is None:
            self.q[qn].append(fn)
            return None
        self.cnt[inc] += amt
        h = self.sems[inc]
        self.q[qn].append(lambda eng, h=h, a=amt, f=fn: f(eng).then_inc(h, a))
        return (inc, self.cnt[inc])

    def opw(self, qn, fn):
        t = self.op(qn, fn, qn)
        self.wait(qn, t)
        return t

    def fence(self, qn):
        self.cnt[qn] += 1
        h = self.sems[qn]
        self.q[qn].append(lambda eng, h=h: eng.nop().then_inc(h, 1))
        return (qn, self.cnt[qn])


def build_nc():
    nc = bass.Bass("TRN2", target_bir_lowering=False,
                   detect_race_conditions=False)
    em = Emitter()

    u_d = nc.dram_tensor("u", [T, D], F32, kind="ExternalInput").ap()
    A_d = nc.dram_tensor("Acol", [64, 1], F32, kind="ExternalInput").ap()
    B_d = nc.dram_tensor("Bmat", [64, D], F32, kind="ExternalInput").ap()
    C_d = nc.dram_tensor("Ccol", [128, 512], F32, kind="ExternalInput").ap()
    Dc_d = nc.dram_tensor("Dcol", [128, 8], F32, kind="ExternalInput").ap()
    ldt_d = nc.dram_tensor("ldtrow", [1, D], F32, kind="ExternalInput").ap()
    g_d = nc.dram_tensor("gcol", [128, 8], F32, kind="ExternalInput").ap()
    bb_d = nc.dram_tensor("bcol", [128, 8], F32, kind="ExternalInput").ap()
    rw_d = nc.dram_tensor("rWp", [128, 64], F32, kind="ExternalInput").ap()
    rb_d = nc.dram_tensor("rbrow", [1, 8], F32, kind="ExternalInput").ap()
    w1_d = nc.dram_tensor("W1", [NE, D, FF], F32, kind="ExternalInput").ap()
    b1_d = nc.dram_tensor("b1col", [128, 128], F32, kind="ExternalInput").ap()
    w2_d = nc.dram_tensor("W2", [NE, FF, D], F32, kind="ExternalInput").ap()
    b2_d = nc.dram_tensor("b2sb", [8, D], F32, kind="ExternalInput").ap()
    id32_d = nc.dram_tensor("ident32", [128, 128], F32, kind="ExternalInput").ap()
    idbf_d = nc.dram_tensor("identbf", [128, 128], BF16, kind="ExternalInput").ap()
    ones32_d = nc.dram_tensor("ones32", [128, 1], F32, kind="ExternalInput").ap()
    onesbf_d = nc.dram_tensor("onesbf", [128, 1], BF16, kind="ExternalInput").ap()
    ones1b_d = nc.dram_tensor("ones1b", [1, 128], BF16, kind="ExternalInput").ap()
    oh8_d = nc.dram_tensor("oneh8", [8, 1024], BF16, kind="ExternalInput").ap()
    oh3_d = nc.dram_tensor("oneh3", [3, 384], BF16, kind="ExternalInput").ap()
    cv_d = nc.dram_tensor("cvals", [128, 2], F32, kind="ExternalInput").ap()
    y_d = nc.dram_tensor("y", [T, D], F32, kind="ExternalOutput").ap()

    ctx = contextlib.ExitStack()
    al = ctx.enter_context

    def sbuf(name, shape, dt):
        return al(nc.sbuf_tensor(name, shape, dt))

    x1 = [sbuf("x1a", [128, CCN * S], F32), sbuf("x1b", [128, CCN * S], F32)]
    unb = sbuf("una", [128, CCN * WCOL], BF16)
    zb = [sbuf("zbufa", [128, CCN * S], BF16), sbuf("zbufb", [128, CCN * S], BF16)]
    hid = sbuf("hid", [128, NFC * S], BF16)
    w1r = sbuf("w1r", [128, 3 * 1024], BF16)
    w2r = sbuf("w2r", [128, 2 * 2048], BF16)
    wrep = sbuf("wrep", [128, S], BF16)
    utm = sbuf("utm", [128, 2 * 1024], BF16)
    otm = sbuf("otm", [128, 1024], F32)
    z0s = sbuf("z0s", [128, CCN * Q], F32)
    tz = sbuf("tz", [128, 128], F32)
    reps = sbuf("reps", [128, max(4 * Q, 512)], F32)
    rows3 = sbuf("rows3", [3, NTILE * 128 + 128], BF16)
    ucols = sbuf("ucols", [128, 32], F32)
    kcol = sbuf("kcol", [128, 512], F32)
    gel = sbuf("gel", [128, 2 * 512], BF16)
    lg = sbuf("lg", [8, S], F32)
    wfm = [sbuf("wfma", [8, S], BF16), sbuf("wfmb", [8, S], BF16)]
    sft8 = sbuf("sft8", [128, 272], F32)
    zrow = sbuf("zrow", [1, 2 * Q], F32)
    zrow2 = sbuf("zrow2", [1, Q], F32)
    zrbf = sbuf("zrbf", [1, 2 * Q], BF16)
    rwp = sbuf("rwp", [128, 64], F32)
    rwp2 = sbuf("rwp2", [128, 64], F32)
    rbc = sbuf("rbc", [1, 8], F32)
    rbcol = sbuf("rbcol", [8, 1], F32)
    b2bf = sbuf("sb_b2bf", [8, D], BF16)
    b1col = sbuf("sb_b1col", [128, 128], F32)
    gcol = sbuf("sb_gcol", [128, 8], F32)
    bcol = sbuf("sb_bcol", [128, 8], F32)
    dcol = sbuf("sb_dcol", [128, 8], F32)
    acol = sbuf("sb_acol", [64, 1], F32)
    id32 = sbuf("sb_id32", [128, 128], F32)
    idbf = sbuf("sb_idbf", [128, 128], BF16)
    ones32 = sbuf("sb_ones32", [128, 1], F32)
    onesbf = sbuf("sb_onesbf", [128, 1], BF16)
    ones1b = sbuf("sb_ones1b", [1, 128], BF16)
    oh8 = sbuf("sb_oh8", [8, 1024], BF16)
    oh3 = sbuf("sb_oh3", [3, 384], BF16)
    ccol = sbuf("ccol", [64, 1], F32)
    dgb = sbuf("dgb", [128, 4 * 128], BF16)

    cval = sbuf("cval", [128, 2], F32)
    smallr = x1[1][0:1, 0:1088]
    xbf8 = utm[:, 0:1024]
    xsq8 = utm[:, 1024:2048]
    rept = reps[:, 2 * Q:2 * Q + 256]

    Cc = x1[1][0:128, 1152:1664]
    Bm = x1[1][0:64, 1664:2688]
    Psb = x1[1][0:64, 2688:3712]
    Abar = x1[1][0:64, 3712:4736]
    cur = x1[1][0:64, 4736:5760]
    cur2 = x1[1][0:64, 5760:6784]

    pW1 = al(nc.psum_tensor("pW1", [128, 1024], F32))
    pW2 = al(nc.psum_tensor("pW2", [128, 1024], F32))
    pT = al(nc.psum_tensor("pTp", [128, 2048], BF16))
    pM = al(nc.psum_tensor("pM", [128, 1024], F32))

    for sname in ("pe", "ve", "sc", "gd", "od",
                  "w1s0", "w1s1", "w1s2", "w2s0", "w2s1", "ut0", "ut1"):
        em.sems[sname] = al(nc.semaphore("s_" + sname))
        em.cnt[sname] = 0

    st = {
        "z0s_ve": None, "z0s_pe": None,
        "rept_ve": None, "reps01_ve": None, "reps23_ve": None,
        "stats_pe": None, "sft_ve": None,
        "pmA": None, "pmB": None,
        "pt": {0: None, 1: None},
        "pw1": {0: None, 1: None},
        "pw2": {0: None, 1: None},
        "gel": {0: None, 1: None},
        "utm": {0: [], 1: []},
        "rows3_sc": None,
        "ucols_pe": {0: None, 1: None},
        "zrbf_pe": None,
        "wrep_sc": None,
        "hid_free_pe": None,
        "experts_pe_done": {},
        "x1_free": {0: None, 1: None},
    }
    last_unb = {}
    conv_done = {}
    recon_done = {}
    zb_ready = {}
    wfm_ready = {}
    tile_dma = {}
    tile_slot = {}
    tile_stats_ve = {}
    stats_done = set()
    tile_seq = {"n": 0}
    pt_seq = {"n": 0}
    w1_slab_tok = []
    w2_slab_tok = []
    w1_gd = []
    w2_gd = []
    w1_emitted = {"n": 0}
    w2_emitted = {"n": 0}
    w1_consumed = {"n": 0}
    w2_consumed = {"n": 0}

    def dma(out_ap, in_ap, sem="gd"):
        return em.op("gp", lambda g, o=out_ap, i=in_ap: g.dma_start(o, i), sem, 16)

    def tap(name, src_ap):
        if name not in _DBG:
            return
        o = nc.dram_tensor("dbg_" + name, list(src_ap.shape), F32,
                           kind="ExternalOutput").ap()
        for s_ in ("pe", "ve", "sc"):
            em.wait("gp", em.tok(s_))
        tk = em.op("gp", lambda g, o=o, a=src_ap: g.dma_start(o, a), "gd", 16)
        for qn in ("pe", "ve", "sc", "sy"):
            em.wait(qn, tk)

    for dst, src in ((acol[:, :], A_d), (Bm, B_d), (Cc, C_d), (dcol[:, :], Dc_d),
                     (smallr[0:1, 64:1088], ldt_d), (gcol[:, :], g_d),
                     (bcol[:, :], bb_d), (rwp[:, :], rw_d), (rbc[:, :], rb_d),
                     (b1col[:, :], b1_d), (b2bf[:, :], b2_d),
                     (id32[:, :], id32_d), (idbf[:, :], idbf_d),
                     (ones32[:, :], ones32_d), (onesbf[:, :], onesbf_d),
                     (ones1b[:, :], ones1b_d), (oh8[:, :], oh8_d),
                     (oh3[:, :], oh3_d), (cval[:, :], cv_d)):
        CONST = dma(dst, src)
    nc.const_aps.register(F32, EPS, cval[:, 0:1])
    nc.const_aps.register(F32, -0.5, cval[:, 1:2])

    def k_setup():
        em.wait("pe", CONST)
        for cc in range(CCN):
            tk = em.op("pe", lambda t, cc=cc: t.matmul(
                pM[0:1, 0:64], ones32[:, :], Cc[:, cc * 64:(cc + 1) * 64],
                start=(cc == 0), stop=(cc == 7)), "pe" if cc == 7 else None)
        em.wait("sc", tk)
        em.opw("sc", lambda s: s.copy(smallr[0:1, 0:64], pM[0:1, 0:64]))
        t_crow = em.fence("sc")
        em.wait("pe", t_crow)
        t_cc = em.op("pe", lambda t: t.transpose(
            pM[0:64, 100:101], smallr[0:1, 0:64], id32[0:1, 0:1]), "pe")
        em.wait("sc", t_cc)
        em.opw("sc", lambda s: s.copy(ccol[:, :], pM[0:64, 100:101]))
        t_ccol = em.fence("sc")
        em.wait("sc", CONST)
        t_dt = em.op("sc", lambda s: s.activation(
            smallr[0:1, 64:1088], smallr[0:1, 64:1088], AF.Exp), "sc")
        em.wait("ve", CONST)
        t_o = em.op("ve", lambda v: v.memset(zrow[0:1, 0:64], 1.0), "ve")
        em.waits("pe", [t_dt, t_o, t_ccol])
        em.op("pe", lambda t: t.matmul(
            pM[0:64, 0:512], zrow[0:1, 0:64], smallr[0:1, 64:576],
            start=True, stop=True))
        t_b2 = em.op("pe", lambda t: t.matmul(
            pM[0:64, 512:1024], zrow[0:1, 0:64], smallr[0:1, 576:1088],
            start=True, stop=True), "pe")
        em.wait("ve", t_b2)
        t_p = em.op("ve", lambda v: v.tensor_scalar(
            Psb, pM[0:64, 0:1024], acol[:, 0:1], None, ALU.mult), "ve")
        em.wait("sc", t_p)
        t_ab = em.op("sc", lambda s: s.activation(Abar, Psb, AF.Exp), "sc")
        t_cur = em.op("ve", lambda v: v.tensor_scalar(
            cur, Bm, ccol[:, 0:1], None, ALU.mult), "ve")
        em.waits("pe", [t_ab, t_cur])
        em.wait("ve", t_ab)
        bufs = [cur, cur2]
        tok_ve = {0: t_cur, 1: None}
        tok_pe = {0: None, 1: None}
        for l in range(LTAP):
            b = l % 2
            em.wait("pe", tok_ve[b])
            for cc in range(CCN):
                tk = em.op("pe", lambda t, cc=cc, l=l, b=b: t.matmul(
                    pW1[:, cc * 64 + l: cc * 64 + l + 1],
                    bufs[b][:, cc * 128:(cc + 1) * 128], ones32[0:64, 0:1],
                    start=True, stop=True), "pe" if cc == 7 else None)
            tok_pe[b] = tk
            if l < LTAP - 1:
                nb = 1 - b
                em.wait("ve", tok_pe[nb])
                tok_ve[nb] = em.op("ve", lambda v, b=b, nb=nb: v.tensor_tensor(
                    bufs[nb], bufs[b], Abar, ALU.mult), "ve")
        em.wait("ve", tk)
        for cc in range(CCN):
            t_df = em.op("ve", lambda v, cc=cc: v.tensor_tensor(
                pW1[:, cc * 64 + 31: cc * 64 + 32],
                pW1[:, cc * 64 + 31: cc * 64 + 32],
                dcol[:, cc:cc + 1], ALU.add), "ve" if cc == 7 else None)
        em.wait("sc", t_df)
        t_k = em.op("sc", lambda s: s.copy(kcol[:, :], pW1[:, 0:512]), "sc")
        em.wait("ve", t_k)
        st["pw1"][0] = t_k
        st["pw1"][1] = t_k

        for cc in range(CCN):
            t_rw = em.op("ve", lambda v, cc=cc: v.tensor_scalar(
                rwp2[:, cc * 8:(cc + 1) * 8], rwp[:, cc * 8:(cc + 1) * 8],
                gcol[:, cc:cc + 1], None, ALU.mult), "ve" if cc == 7 else None)
        for cc in range(CCN):
            t_rb = em.op("pe", lambda t, cc=cc: t.matmul(
                pM[0:1, 200:208], bcol[:, cc:cc + 1], rwp[:, cc * 8:(cc + 1) * 8],
                start=(cc == 0), stop=(cc == 7)), "pe" if cc == 7 else None)
        em.wait("ve", t_rb)
        em.opw("ve", lambda v: v.tensor_tensor(
            rbc[:, :], pM[0:1, 200:208], rbc[:, :], ALU.add))
        t_rb2 = em.fence("ve")
        em.wait("pe", t_rb2)
        t_rb3 = em.op("pe", lambda t: t.transpose(
            pM[0:8, 300:301], rbc[0:1, 0:8], id32[0:1, 0:1]), "pe")
        em.wait("sc", t_rb3)
        em.opw("sc", lambda s: s.copy(rbcol[:, :], pM[0:8, 300:301]))
        t_rb4 = em.fence("sc")
        st["pmA"] = t_rb4
        st["pmB"] = t_rb4
        st["hid_free_pe"] = em.tok("pe")
        _ = t_rw

    k_setup()

    def win_tiles(w):
        return [i for i in range(8 * w - 1, 8 * w + 9) if 0 <= i < NTILE]

    def emit_tile_dma(w, i):
        slot = tile_seq["n"] % 2
        tile_seq["n"] += 1
        tile_slot[(w, i)] = slot
        for t in st["utm"][slot]:
            em.wait("gp", t)
        tile_dma[(w, i)] = dma(utm[:, slot * 1024:(slot + 1) * 1024],
                               u_d[i * 128:(i + 1) * 128, :], "ut%d" % slot)

    stats_seq = {"n": 0}
    stats_tok = {}

    def emit_tile_stats_ve(w, i):
        slot = tile_slot[(w, i)]
        sl = stats_seq["n"] % 2
        stats_seq["n"] += 1
        us = utm[:, slot * 1024:(slot + 1) * 1024]
        uc = ucols[:, sl * 16:(sl + 1) * 16]
        em.wait("ve", tile_dma[(w, i)])
        em.wait("ve", st["ucols_pe"][sl])
        em.wait("ve", st.get("z0s_pe"))
        em.opw("ve", lambda v, uc=uc, us=us: v.reduce_sum(
            uc[:, 0:1], us, axis=AX.X))
        t_sq = em.op("ve", lambda v, us=us: v.tensor_tensor(
            z0s[:, 0:1024], us, us, ALU.mult), "ve")
        tile_stats_ve[(w, i)] = (t_sq,)
        em.opw("ve", lambda v, uc=uc: v.reduce_sum(
            uc[:, 1:2], z0s[:, 0:1024], axis=AX.X))
        em.opw("ve", lambda v, uc=uc: v.tensor_scalar(
            uc[:, 8:9], uc[:, 0:1], 1.0 / D, None, ALU.mult))
        em.opw("ve", lambda v, uc=uc: v.tensor_tensor(
            uc[:, 3:4], uc[:, 8:9], uc[:, 8:9], ALU.mult))
        em.opw("ve", lambda v, uc=uc: v.scalar_tensor_tensor(
            uc[:, 4:5], uc[:, 1:2], 1.0 / D, uc[:, 3:4],
            ALU.mult, ALU.subtract))
        em.opw("ve", lambda v, uc=uc: v.tensor_scalar(
            uc[:, 4:5], uc[:, 4:5], EPS, None, ALU.add))
        em.opw("ve", lambda v, uc=uc: v.reciprocal(uc[:, 5:6], uc[:, 4:5]))
        for _ in range(5):
            em.opw("ve", lambda v, uc=uc: v.tensor_tensor(
                uc[:, 6:7], uc[:, 5:6], uc[:, 5:6], ALU.mult))
            em.opw("ve", lambda v, uc=uc: v.tensor_tensor(
                uc[:, 6:7], uc[:, 6:7], uc[:, 4:5], ALU.mult))
            em.opw("ve", lambda v, uc=uc: v.tensor_scalar(
                uc[:, 6:7], uc[:, 6:7], -0.5, 1.5, ALU.mult, ALU.add))
            em.opw("ve", lambda v, uc=uc: v.tensor_tensor(
                uc[:, 5:6], uc[:, 5:6], uc[:, 6:7], ALU.mult))
        em.opw("ve", lambda v, uc=uc: v.tensor_tensor(
            uc[:, 9:10], uc[:, 4:5], uc[:, 5:6], ALU.mult))
        em.opw("ve", lambda v, uc=uc: v.tensor_copy(uc[:, 10:11], uc[:, 5:6]))
        t_uc = em.fence("ve")
        stats_tok[(w, i)] = (t_uc, sl)

    def emit_tile_stats_pe(w, i):
        t_uc, sl = stats_tok.pop((w, i))
        uc = ucols[:, sl * 16:(sl + 1) * 16]
        em.waits("pe", [t_uc, st["pmA"]])
        t_r = em.op("pe", lambda t, uc=uc: t.matmul(
            pM[0:3, 0:128], uc[:, 8:11], id32[:, :], start=True, stop=True), "pe")
        st["ucols_pe"][sl] = t_r
        em.wait("sc", t_r)
        t_s = em.op("sc", lambda s, i=i: s.copy(
            rows3[0:3, 32 + i * 128: 32 + (i + 1) * 128], pM[0:3, 0:128]), "sc")
        st["rows3_sc"] = t_s
        st["pmA"] = t_s
        stats_done.add(i)

    def emit_tile_tp_un(w, i):
        slot = tile_slot[(w, i)]
        half = pt_seq["n"] % 2
        pt_seq["n"] += 1
        us = utm[:, slot * 1024:(slot + 1) * 1024]
        a = max(i * 128, w * S - 32)
        b = min(i * 128 + 128, w * S + S + 32)
        sj0 = a - i * 128
        wd0 = a - (w * S - 32)
        n = b - a
        em.waits("pe", [st["pt"][half], tile_dma[(w, i)]])
        for cc in range(CCN):
            t_tp = em.op("pe", lambda t, cc=cc, half=half, us=us: t.transpose(
                pT[:, half * 1024 + cc * 128: half * 1024 + (cc + 1) * 128],
                us[:, cc * 128:(cc + 1) * 128], idbf[:, :]),
                "pe" if cc == 7 else None)
        st["utm"][slot] = [t_tp]
        tsv = tile_stats_ve.get((w, i))
        if tsv is not None:
            st["utm"][slot].extend(tsv)
        em.waits("pe", [st["rows3_sc"], st["pmB"]])
        rc = 32 + a
        em.op("pe", lambda t, rc=rc, n=n: t.matmul(
            pM[:, 512:512 + n], oh3[0:3, 0:128], rows3[0:3, rc:rc + n],
            start=True, stop=True))
        t_bc = em.op("pe", lambda t, rc=rc, n=n: t.matmul(
            pM[:, 768:768 + n], oh3[0:3, 256:384], rows3[0:3, rc:rc + n],
            start=True, stop=True), "pe")
        em.waits("sc", [t_bc, st.get("rept_ve")])
        em.op("sc", lambda s, n=n: s.copy(rept[:, 0:n], pM[:, 512:512 + n]))
        t_rp = em.op("sc", lambda s, n=n: s.copy(
            rept[:, 128:128 + n], pM[:, 768:768 + n]), "sc")
        st["pmB"] = t_rp
        em.waits("ve", [t_tp, t_rp])
        em.wait("ve", conv_done.get(w - 1))
        for cc in range(CCN):
            em.op("ve", lambda v, cc=cc, half=half, sj0=sj0, n=n: v.tensor_tensor(
                tz[:, 0:n],
                pT[:, half * 1024 + cc * 128 + sj0: half * 1024 + cc * 128 + sj0 + n],
                rept[:, 0:n], ALU.subtract))
            t_un = em.op("ve", lambda v, cc=cc, wd0=wd0, n=n: v.tensor_tensor(
                unb[:, cc * WCOL + wd0: cc * WCOL + wd0 + n],
                tz[:, 0:n], rept[:, 128:128 + n], ALU.mult),
                "ve" if cc == 7 else None)
        st["pt"][half] = t_un
        st["rept_ve"] = t_un
        st["reps23_ve"] = t_un
        last_unb[w] = t_un

    def emit_pads(w):
        if w == 0:
            for cc in range(CCN):
                t_p = em.op("ve", lambda v, cc=cc: v.memset(
                    unb[:, cc * WCOL: cc * WCOL + 32], 0.0),
                    "ve" if cc == 7 else None)
            last_unb[0] = t_p
        if w == NSUP - 1:
            em.wait("ve", conv_done.get(w - 1))
            for cc in range(CCN):
                t_p = em.op("ve", lambda v, cc=cc: v.memset(
                    unb[:, cc * WCOL + 1056: cc * WCOL + 1088], 0.0),
                    "ve" if cc == 7 else None)
            last_unb[w] = t_p

    dg_pe = [None, None, None, None]

    def emit_conv_pe(w):
        par = w % 2
        em.wait("pe", last_unb[w])
        em.wait("ve", st["x1_free"][par])
        nd = {"n": 0}
        for cc in range(CCN):
            em.waits("pe", [st["pw1"][0], st["pw1"][1]])
            for l in range(LTAP):
                slot = nd["n"] % 4
                nd["n"] += 1
                em.wait("ve", dg_pe[slot])
                kb = kcol[:, cc * 64 + l: cc * 64 + l + 1].to_broadcast([128, 128])
                t_d = em.op("ve", lambda v, slot=slot, kb=kb: v.tensor_tensor(
                    dgb[:, slot * 128:(slot + 1) * 128], idbf[:, :], kb, ALU.mult),
                    "ve")
                em.wait("pe", t_d)
                for half in range(2):
                    src = unb[:, cc * WCOL + (63 - l) + half * 512:
                              cc * WCOL + (63 - l) + half * 512 + 512]
                    t_mm = em.op("pe", lambda t, slot=slot, src=src, half=half,
                                 st_=(l == 0), sp_=(l == LTAP - 1):
                                 t.matmul(pW1[:, half * 512:(half + 1) * 512],
                                          dgb[:, slot * 128:(slot + 1) * 128], src,
                                          start=st_, stop=sp_),
                                 "pe" if half == 1 else None)
                dg_pe[slot] = t_mm
            em.wait("ve", t_mm)
            for half in range(2):
                t_ev = em.op("ve", lambda v, cc=cc, half=half, par=par:
                             v.tensor_copy(
                                 x1[par][:, cc * S + half * 512:
                                         cc * S + half * 512 + 512],
                                 pW1[:, half * 512:(half + 1) * 512]), "ve")
            st["pw1"][0] = t_ev
            st["pw1"][1] = t_ev
        conv_done[w] = t_ev

    def emit_conv_group(w, cc, g):
        par = w % 2
        if cc == 0 and g == 0:
            em.wait("ve", st["x1_free"][par])
            em.wait("ve", last_unb[w])
            em.wait("ve", st["hid_free_pe"])
        for l in range(g * TAPG, (g + 1) * TAPG):
            src = unb[:, cc * WCOL + (63 - l): cc * WCOL + (63 - l) + S]
            dst = x1[par][:, cc * S:(cc + 1) * S]
            sc_ap = kcol[:, cc * 64 + l: cc * 64 + l + 1]
            last = (l == LTAP - 1)
            if l == 0:
                em.op("ve", lambda v, src=src, dst=dst, sc_ap=sc_ap: v.tensor_scalar(
                    dst, src, sc_ap, None, ALU.mult))
            else:
                tk = em.op("ve", lambda v, src=src, dst=dst, sc_ap=sc_ap:
                           v.scalar_tensor_tensor(dst, src, sc_ap, dst,
                                                  ALU.mult, ALU.add),
                           "ve" if last else None)
                if last and cc == CCN - 1:
                    conv_done[w] = tk

    recon_prev = {}

    def emit_recon_q(w, qi):
        par = w % 2
        t0 = qi * Q
        rc = 32 + w * S + t0
        em.waits("pe", [st["pmA"], st["rows3_sc"]])
        em.op("pe", lambda t, rc=rc: t.matmul(
            pM[:, 0:Q], oh3[0:3, 0:128], rows3[0:3, rc:rc + Q],
            start=True, stop=True))
        t_b = em.op("pe", lambda t, rc=rc: t.matmul(
            pM[:, Q:2 * Q], oh3[0:3, 128:256], rows3[0:3, rc:rc + Q],
            start=True, stop=True), "pe")
        em.waits("sc", [t_b, recon_prev.get(w), st.get("reps01_ve")])
        em.op("sc", lambda s: s.copy(reps[:, 0:Q], pM[:, 0:Q]))
        t_e = em.op("sc", lambda s: s.copy(reps[:, Q:2 * Q], pM[:, Q:2 * Q]), "sc")
        st["pmA"] = t_e
        em.waits("ve", [t_e, conv_done[w]])
        un3 = unb[:, :].rearrange("p (c t) -> p c t", c=8)[
            :, :, 32 + t0: 32 + t0 + Q]
        xq3 = x1[par][:, :].rearrange("p (c t) -> p c t", c=8)[
            :, :, t0:t0 + Q]
        iv_b = reps[:, Q:2 * Q].unsqueeze(1).to_broadcast([128, 8, Q])
        mu_b = reps[:, 0:Q].unsqueeze(1).to_broadcast([128, 8, Q])
        tz3 = z0s[:, :].rearrange("p (c t) -> p c t", c=8)
        em.op("ve", lambda v, un3=un3, iv_b=iv_b: v.tensor_tensor(
            tz3, un3, iv_b, ALU.mult))
        em.op("ve", lambda v, xq3=xq3: v.tensor_tensor(
            xq3, xq3, tz3, ALU.add))
        t_r = em.op("ve", lambda v, xq3=xq3, mu_b=mu_b: v.tensor_tensor(
            xq3, xq3, mu_b, ALU.add), "ve")
        recon_prev[w] = t_r
        if qi == NQ - 1:
            recon_done[w] = t_r
            st["reps01_ve"] = t_r

    WQ = {w: [] for w in range(NSUP)}
    WQ_pace = {}

    def build_wq(w):
        tasks = []
        fin = []

        def finish_one():
            j, is_new = fin.pop(0)
            if is_new:
                tasks.append(lambda w=w, j=j: emit_tile_stats_pe(w, j))
            tasks.append(lambda w=w, j=j: emit_tile_tp_un(w, j))

        for i in win_tiles(w):
            tasks.append(lambda w=w, i=i: emit_tile_dma(w, i))
            if i not in stats_done:
                tasks.append(lambda w=w, i=i: emit_tile_stats_ve(w, i))
                fin.append((i, True))
            else:
                fin.append((i, False))
            while len(fin) > 1:
                finish_one()
        while fin:
            finish_one()
        tasks.append(lambda w=w: emit_pads(w))
        if w == 0:
            tasks.append(lambda w=w: emit_conv_pe(w))
        else:
            for cc in range(CCN):
                for g in range(LTAP // TAPG):
                    tasks.append(lambda w=w, cc=cc, g=g: emit_conv_group(w, cc, g))
        if OVERLAP_RZ:
            for qi in range(NQ):
                tasks.append(lambda w=w, qi=qi: emit_recon_q(w, qi))
            tasks.extend(zphase_tasks(w))
        return tasks

    def emit_rz(w):
        if OVERLAP_RZ:
            return
        for qi in range(NQ):
            emit_recon_q(w, qi)
        for fn in zphase_tasks(w):
            fn()

    PUMP_SLOTS = NE * 48

    def pump(w, n):
        if w is None or w >= NSUP:
            return
        q = WQ[w]
        if n >= 10 ** 5:
            while q:
                q.pop(0)()
            return
        pc = WQ_pace.setdefault(w, {"credit": 0.0, "rate": None})
        if pc["rate"] is None:
            pc["rate"] = (len(q) + 8) / float(PUMP_SLOTS)
        pc["credit"] += n * pc["rate"]
        while pc["credit"] >= 1.0 and q:
            q.pop(0)()
            pc["credit"] -= 1.0

    zst = {}

    def stats_pass(src3, t_prev):
        em.waits("sc", [st["stats_pe"], t_prev])
        for tk_ in st["utm"][0] + st["utm"][1]:
            em.wait("sc", tk_)
        em.op("sc", lambda s, src3=src3: s.copy(
            xbf8[:, :].rearrange("p (c t) -> p c t", c=8), src3))
        t_c = em.op("sc", lambda s, src3=src3: s.activation(
            xsq8[:, :].rearrange("p (c t) -> p c t", c=8), src3,
            AF.Square), "sc")
        em.waits("pe", [t_c, st["pmA"], st["pmB"]])
        for cc in range(CCN):
            em.op("pe", lambda t, cc=cc: t.matmul(
                pM[0:1, 0:Q], onesbf[:, :], xbf8[:, cc * Q:(cc + 1) * Q],
                start=(cc == 0), stop=(cc == 7)))
            t_m = em.op("pe", lambda t, cc=cc: t.matmul(
                pM[0:1, 512:512 + Q], onesbf[:, :], xsq8[:, cc * Q:(cc + 1) * Q],
                start=(cc == 0), stop=(cc == 7)), "pe")
        st["stats_pe"] = t_m
        return t_m

    def rows_from_stats(t_m):
        em.wait("sc", t_m)
        em.op("sc", lambda s: s.copy(zrow[0:1, 0:Q], pM[0:1, 0:Q]))
        em.op("sc", lambda s: s.copy(zrow[0:1, Q:2 * Q], pM[0:1, 512:512 + Q]))
        em.op("sc", lambda s: s.mul(zrow[0:1, 0:Q], zrow[0:1, 0:Q], 1.0 / D))
        em.op("sc", lambda s: s.activation(
            zrow2[0:1, 0:Q], zrow[0:1, 0:Q], AF.Square), "sc")
        t_r = em.fence("sc")
        em.wait("ve", t_r)
        em.op("ve", lambda v: v.scalar_tensor_tensor(
            zrow[0:1, Q:2 * Q], zrow[0:1, Q:2 * Q], 1.0 / D, zrow2[0:1, 0:Q],
            ALU.mult, ALU.subtract), "ve")
        t_v = em.fence("ve")
        em.wait("sc", t_v)
        em.op("sc", lambda s: s.activation(
            zrow2[0:1, 0:Q], zrow[0:1, Q:2 * Q], AF.Ln, bias=EPS))
        em.op("sc", lambda s: s.activation(
            zrow[0:1, Q:2 * Q], zrow2[0:1, 0:Q], AF.Exp, scale=-0.5))
        em.wait("sc", st["zrbf_pe"])
        em.op("sc", lambda s: s.copy(
            zrbf[0:1, 0:2 * Q], zrow[0:1, 0:2 * Q]), "sc")
        return em.fence("sc")

    def _zx3(k):
        return x1[k % 2][:, :].rearrange("p (c t) -> p c t", c=8)

    _z3g = None

    def z_ln2_stats(k, qi):
        qs = qi * Q
        xq = _zx3(k)[:, :, qs:qs + Q]
        zst["xq"] = xq
        t_prev = zst.get("t_zw") if qi > 0 else recon_done[k]
        zst["t_m"] = stats_pass(xq, t_prev)

    def z_ln2_apply(k, qi):
        z3 = z0s[:, :].rearrange("p (c t) -> p c t", c=8)
        st["pmA"] = ("sc", em.cnt["sc"] + 2)
        st["pmB"] = ("sc", em.cnt["sc"] + 2)
        t_zb = rows_from_stats(zst["t_m"])
        em.waits("pe", [t_zb, st["pmA"]])
        em.op("pe", lambda t: t.matmul(
            pM[:, 0:Q], ones1b[0:1, 0:128], zrbf[0:1, 0:Q],
            start=True, stop=True))
        t_b = em.op("pe", lambda t: t.matmul(
            pM[:, Q:2 * Q], ones1b[0:1, 0:128], zrbf[0:1, Q:2 * Q],
            start=True, stop=True), "pe")
        st["zrbf_pe"] = t_b
        em.waits("sc", [t_b, st["reps01_ve"]])
        t_e = em.op("sc", lambda s: s.copy(reps[:, 0:2 * Q], pM[:, 0:2 * Q]), "sc")
        st["pmA"] = t_e
        xq = zst["xq"]
        em.waits("ve", [t_e, zst.get("t_rt")])
        mu_b = reps[:, 0:Q].unsqueeze(1).to_broadcast([128, 8, Q])
        s_b = reps[:, Q:2 * Q].unsqueeze(1).to_broadcast([128, 8, Q])
        em.op("ve", lambda v, xq=xq, mu_b=mu_b, z3=z3: v.tensor_tensor(
            z3, xq, mu_b, ALU.subtract))
        em.op("ve", lambda v, s_b=s_b, z3=z3: v.tensor_tensor(
            z3, z3, s_b, ALU.mult))
        t_z0 = em.op("ve", lambda v, xq=xq, z3=z3: v.tensor_tensor(
            xq, xq, z3, ALU.add), "ve")
        st["reps01_ve"] = t_z0
        zst["t_z0"] = t_z0

    def z_ln3_stats(k, qi):
        z3 = z0s[:, :].rearrange("p (c t) -> p c t", c=8)
        zst["t_m"] = stats_pass(z3, zst["t_z0"])

    def z_ln3_apply(k, qi):
        z3 = z0s[:, :].rearrange("p (c t) -> p c t", c=8)
        st["pmA"] = ("sc", em.cnt["sc"] + 2)
        st["pmB"] = ("sc", em.cnt["sc"] + 2)
        t_zb = rows_from_stats(zst["t_m"])
        em.waits("pe", [t_zb, st["pmA"], st["pmB"]])
        em.op("pe", lambda t: t.matmul(
            pM[:, 0:Q], ones1b[0:1, 0:128], zrbf[0:1, 0:Q],
            start=True, stop=True))
        t_b = em.op("pe", lambda t: t.matmul(
            pM[:, Q:2 * Q], ones1b[0:1, 0:128], zrbf[0:1, Q:2 * Q],
            start=True, stop=True), "pe")
        st["zrbf_pe"] = t_b
        em.waits("sc", [t_b, st["reps23_ve"]])
        t_e = em.op("sc", lambda s: s.copy(
            reps[:, 2 * Q:4 * Q], pM[:, 0:2 * Q]), "sc")
        st["pmA"] = t_e
        em.wait("ve", t_e)
        mu3_b = reps[:, 2 * Q:3 * Q].unsqueeze(1).to_broadcast([128, 8, Q])
        s3_b = reps[:, 3 * Q:4 * Q].unsqueeze(1).to_broadcast([128, 8, Q])
        em.op("ve", lambda v, mu3_b=mu3_b, z3=z3: v.tensor_tensor(
            z3, z3, mu3_b, ALU.subtract))
        t_n = em.op("ve", lambda v, s3_b=s3_b, z3=z3: v.tensor_tensor(
            z3, z3, s3_b, ALU.mult), "ve")
        st["reps23_ve"] = t_n
        zst["t_n"] = t_n

    def z_route(k, qi):
        par = k % 2
        qs = qi * Q
        z3 = z0s[:, :].rearrange("p (c t) -> p c t", c=8)
        zb3 = zb[par][:, :].rearrange("p (c t) -> p c t", c=8)
        zbq = zb3[:, :, qs:qs + Q]
        g_b = gcol[:, :].unsqueeze(2).to_broadcast([128, 8, Q])
        b_b = bcol[:, :].unsqueeze(2).to_broadcast([128, 8, Q])
        em.wait("ve", st["experts_pe_done"].get(k - 2))
        em.op("ve", lambda v, zbq=zbq, g_b=g_b, z3=z3: v.tensor_tensor(
            zbq, z3, g_b, ALU.mult))
        t_zw = em.op("ve", lambda v, zbq=zbq, b_b=b_b: v.tensor_tensor(
            zbq, zbq, b_b, ALU.add), "ve")
        zst["t_zw"] = t_zw
        if qi == NQ - 1:
            zb_ready[k] = t_zw
        em.wait("pe", zst["t_n"])
        em.wait("pe", st["pmA"])
        for cc in range(CCN):
            t_rt = em.op("pe", lambda t, cc=cc: t.matmul(
                pM[0:8, 0:Q], rwp2[:, cc * 8:(cc + 1) * 8],
                z0s[:, cc * Q:(cc + 1) * Q],
                start=(cc == 0), stop=(cc == 7)), "pe" if cc == 7 else None)
        zst["t_rt"] = t_rt
        em.wait("ve", t_rt)
        t_lg = em.op("ve", lambda v, qs=qs: v.tensor_scalar(
            lg[0:8, qs:qs + Q], pM[0:8, 0:Q], rbcol[0:8, 0:1], None, ALU.add),
            "ve")
        st["pmA"] = t_lg
        zst["t_lg"] = t_lg

    def z_soft1(k):
        em.waits("pe", [zst["t_lg"], st["pmB"]])
        for ti in range(8):
            t_tp = em.op("pe", lambda t, ti=ti: t.transpose(
                pM[:, 512 + 8 * ti: 520 + 8 * ti], lg[0:8, ti * 128:(ti + 1) * 128],
                id32[0:8, 0:8]), "pe" if ti == 7 else None)
        em.waits("sc", [t_tp, st["sft_ve"]])
        em.opw("sc", lambda s: s.copy(sft8[:, 0:64], pM[:, 512:576]))
        t_lt = em.fence("sc")
        st["pmB"] = t_lt
        em.wait("ve", t_lt)
        s8 = sft8[:, 0:64].rearrange("p (i e) -> p i e", i=8)
        m8 = sft8[:, 256:264]
        em.opw("ve", lambda v, s8=s8, m8=m8: v.reduce_max(
            m8.unsqueeze(2), s8, axis=AX.X, negate=True))
        em.opw("ve", lambda v, s8=s8, m8=m8: v.tensor_tensor(
            s8, s8, m8.unsqueeze(2).to_broadcast([128, 8, 8]), ALU.add))
        zst["t_sm"] = em.fence("ve")

    def z_soft2(k):
        s8 = sft8[:, 0:64].rearrange("p (i e) -> p i e", i=8)
        e8 = sft8[:, 64:128].rearrange("p (i e) -> p i e", i=8)
        p8 = sft8[:, 128:192].rearrange("p (i e) -> p i e", i=8)
        k8 = sft8[:, 192:256].rearrange("p (i e) -> p i e", i=8)
        m8 = sft8[:, 256:264]
        c8 = sft8[:, 264:272]
        em.wait("sc", zst["t_sm"])
        em.opw("sc", lambda s: s.activation(e8, s8, AF.Exp))
        t_ex = em.fence("sc")
        em.wait("ve", t_ex)
        em.opw("ve", lambda v: v.reduce_sum(c8.unsqueeze(2), e8, axis=AX.X))
        em.opw("ve", lambda v: v.reciprocal(m8, c8))
        em.opw("ve", lambda v: v.tensor_tensor(
            p8, e8, m8.unsqueeze(2).to_broadcast([128, 8, 8]), ALU.mult))
        em.opw("ve", lambda v: v.reduce_max(
            m8.unsqueeze(2), p8, axis=AX.X, negate=True))
        em.opw("ve", lambda v: v.tensor_tensor(
            k8, p8, m8.unsqueeze(2).to_broadcast([128, 8, 8]), ALU.add))
        em.opw("ve", lambda v: v.tensor_scalar(
            k8, k8, 0.0, None, ALU.is_ge))
        em.opw("ve", lambda v: v.scalar_tensor_tensor(
            k8, k8, -1e30, p8, ALU.mult, ALU.add))
        em.opw("ve", lambda v: v.reduce_max(
            m8.unsqueeze(2), k8, axis=AX.X, negate=True))
        em.opw("ve", lambda v: v.tensor_tensor(
            k8, p8, m8.unsqueeze(2).to_broadcast([128, 8, 8]), ALU.add))
        em.opw("ve", lambda v: v.tensor_scalar(
            k8, k8, 0.0, None, ALU.is_ge))
        em.opw("ve", lambda v: v.tensor_tensor(k8, k8, p8, ALU.mult))
        em.opw("ve", lambda v: v.reduce_sum(c8.unsqueeze(2), k8, axis=AX.X))
        em.opw("ve", lambda v: v.tensor_scalar(c8, c8, 1e-8, None, ALU.max))
        em.opw("ve", lambda v: v.reciprocal(m8, c8))
        em.opw("ve", lambda v: v.tensor_tensor(
            k8, k8, m8.unsqueeze(2).to_broadcast([128, 8, 8]), ALU.mult))
        t_w = em.fence("ve")
        st["sft_ve"] = t_w
        zst["t_w"] = t_w

    def z_soft3(k):
        par = k % 2
        for ti in range(8):
            em.waits("pe", [zst["t_w"], st["pmB"]])
            t_tw = em.op("pe", lambda t, ti=ti: t.transpose(
                pM[0:8, 512:640], sft8[:, 192 + 8 * ti: 200 + 8 * ti], id32[:, :]),
                "pe")
            em.wait("sc", t_tw)
            em.wait("sc", st["experts_pe_done"].get(k - 2))
            t_wf = em.op("sc", lambda s, ti=ti, par=par: s.copy(
                wfm[par][0:8, ti * 128:(ti + 1) * 128], pM[0:8, 512:640]), "sc")
            st["pmB"] = t_wf
        wfm_ready[k] = t_wf
        st["z0s_pe"] = em.tok("pe")
        st["rept_ve"] = em.tok("ve")
        st["reps01_ve"] = em.tok("ve")
        st["reps23_ve"] = em.tok("ve")
        st["utm"][0].append(st["stats_pe"])
        st["utm"][1].append(st["stats_pe"])

    def zphase_tasks(k):
        ts_ = []
        for qi in range(NQ):
            ts_.append(lambda k=k, qi=qi: z_ln2_stats(k, qi))
            ts_.append(lambda k=k, qi=qi: z_ln2_apply(k, qi))
            ts_.append(lambda k=k, qi=qi: z_ln3_stats(k, qi))
            ts_.append(lambda k=k, qi=qi: z_ln3_apply(k, qi))
            ts_.append(lambda k=k, qi=qi: z_route(k, qi))
        ts_.append(lambda k=k: z_soft1(k))
        ts_.append(lambda k=k: z_soft2(k))
        ts_.append(lambda k=k: z_soft3(k))
        return ts_

    W1TOT = NSUP * NE * NFC
    W2TOT = NSUP * NE * NMC

    def prefetch_w1(upto):
        while w1_emitted["n"] <= min(upto, W1TOT - 1):
            n = w1_emitted["n"]
            slot = n % 3
            if n >= 3:
                em.wait("gp", w1_slab_tok[n - 3])
            e = (n // NFC) % NE
            fc = n % NFC
            src = w1_d.rearrange("e (hc p) f -> e p hc f", p=128)[
                e, :, :, fc * 128:(fc + 1) * 128]
            w1_gd.append(dma(w1r[:, slot * 1024:(slot + 1) * 1024], src,
                             "w1s%d" % slot))
            w1_emitted["n"] += 1

    def prefetch_w2(upto):
        while w2_emitted["n"] <= min(upto, W2TOT - 1):
            n = w2_emitted["n"]
            slot = n % 2
            if n >= 2:
                em.wait("gp", w2_slab_tok[n - 2])
            e = (n // NMC) % NE
            mc = n % NMC
            src = w2_d.rearrange("e (fc p) m -> e p fc m", p=128)[
                e, :, :, mc * 128:(mc + 1) * 128]
            w2_gd.append(dma(w2r[:, slot * 2048:(slot + 1) * 2048], src,
                             "w2s%d" % slot))
            w2_emitted["n"] += 1

    def emit_expert(k, e, pumpw):
        par = k % 2
        em.waits("pe", [wfm_ready[k], st["pw2"][0], st["pw2"][1]])
        em.op("pe", lambda t, e=e, par=par: t.matmul(
            pW2[:, 0:512], oh8[0:8, e * 128:(e + 1) * 128], wfm[par][0:8, 0:512],
            start=True, stop=True))
        t_wr = em.op("pe", lambda t, e=e, par=par: t.matmul(
            pW2[:, 512:1024], oh8[0:8, e * 128:(e + 1) * 128],
            wfm[par][0:8, 512:1024], start=True, stop=True), "pe")
        em.waits("sc", [t_wr, st["wrep_sc"]])
        t_we = em.op("sc", lambda s: s.copy(wrep[:, :], pW2[:, 0:1024]), "sc")
        st["pw2"][0] = t_we
        st["pw2"][1] = t_we
        em.wait("pe", zb_ready[k])
        for fc in range(NFC):
            n1 = w1_consumed["n"]
            slot = n1 % 3
            prefetch_w1(n1 + 2)
            prefetch_w2(w2_consumed["n"] + 1)
            em.wait("pe", w1_gd[n1])
            for half in range(2):
                pp = (fc * 2 + half) % 2
                em.wait("pe", st["pw1"][pp])
                for hc in range(CCN):
                    t_mm = em.op("pe", lambda t, slot=slot, hc=hc, half=half,
                                 pp=pp, par=par:
                                 t.matmul(pW1[:, pp * 512:(pp + 1) * 512],
                                          w1r[:, slot * 1024 + hc * 128:
                                              slot * 1024 + (hc + 1) * 128],
                                          zb[par][:, hc * S + half * 512:
                                                  hc * S + half * 512 + 512],
                                          start=(hc == 0), stop=(hc == 7)),
                                 "pe" if hc == 7 else None)
                gg = pp
                em.waits("sc", [t_mm, st["gel"][gg]])
                t_g = em.op("sc", lambda s, pp=pp, gg=gg, fc=fc, e=e: s.activation(
                    gel[:, gg * 512:(gg + 1) * 512],
                    pW1[:, pp * 512:(pp + 1) * 512],
                    AF.Gelu, bias=b1col[:, fc * 8 + e: fc * 8 + e + 1]), "sc")
                st["pw1"][pp] = t_g
                em.waits("ve", [t_g, t_we, st["hid_free_pe"]])
                t_h = em.op("ve", lambda v, gg=gg, fc=fc, half=half: v.tensor_tensor(
                    hid[:, fc * S + half * 512: fc * S + half * 512 + 512],
                    gel[:, gg * 512:(gg + 1) * 512],
                    wrep[:, half * 512:(half + 1) * 512], ALU.mult), "ve")
                st["gel"][gg] = t_h
                pump(pumpw, 1)
            w1_slab_tok.append(em.tok("pe"))
            w1_consumed["n"] += 1
        hid_ready = t_h
        st["wrep_sc"] = em.tok("ve")
        em.wait("pe", hid_ready)
        for mc in range(NMC):
            n2 = w2_consumed["n"]
            slot = n2 % 2
            prefetch_w2(n2 + 1)
            prefetch_w1(w1_consumed["n"] + 2)
            em.wait("pe", w2_gd[n2])
            for half in range(2):
                pp = (mc * 2 + half) % 2
                em.wait("pe", st["pw2"][pp])
                if e == 0:
                    em.op("pe", lambda t, mc=mc, half=half, pp=pp, par=par: t.matmul(
                        pW2[:, pp * 512:(pp + 1) * 512],
                        b2bf[0:8, mc * 128:(mc + 1) * 128],
                        wfm[par][0:8, half * 512:(half + 1) * 512],
                        start=True, stop=False))
                for fc in range(NFC):
                    t_mm = em.op("pe", lambda t, slot=slot, fc=fc, half=half, pp=pp,
                                 first=(e != 0 and fc == 0):
                                 t.matmul(pW2[:, pp * 512:(pp + 1) * 512],
                                          w2r[:, slot * 2048 + fc * 128:
                                              slot * 2048 + (fc + 1) * 128],
                                          hid[:, fc * S + half * 512:
                                              fc * S + half * 512 + 512],
                                          start=first, stop=(fc == NFC - 1)),
                                 "pe" if fc == NFC - 1 else None)
                em.wait("ve", t_mm)
                t_ev = em.op("ve", lambda v, mc=mc, half=half, pp=pp, par=par:
                             v.tensor_tensor(
                                 x1[par][:, mc * S + half * 512:
                                         mc * S + half * 512 + 512],
                                 x1[par][:, mc * S + half * 512:
                                         mc * S + half * 512 + 512],
                                 pW2[:, pp * 512:(pp + 1) * 512], ALU.add), "ve")
                st["pw2"][pp] = t_ev
                pump(pumpw, 1)
            w2_slab_tok.append(em.tok("pe"))
            w2_consumed["n"] += 1
        st["hid_free_pe"] = em.tok("pe")
        return t_ev

    otm_half = {0: None, 1: None}

    def emit_outphase(k, x1_done):
        par = k % 2
        for j in range(8):
            for half in range(2):
                em.waits("pe", [x1_done, st["pw2"][half]])
                for c4 in range(4):
                    cc = half * 4 + c4
                    t_tp = em.op("pe", lambda t, cc=cc, c4=c4, j=j, par=par,
                                 half=half: t.transpose(
                        pW2[:, half * 512 + c4 * 128: half * 512 + c4 * 128 + 128],
                        x1[par][:, cc * S + j * 128: cc * S + j * 128 + 128],
                        id32[:, :]), "pe" if c4 == 3 else None)
                em.waits("sc", [t_tp, otm_half[half]])
                t_e = em.op("sc", lambda s, half=half: s.copy(
                    otm[:, half * 512:(half + 1) * 512],
                    pW2[:, half * 512:(half + 1) * 512]), "sc")
                st["pw2"][half] = t_e
                em.wait("sy", t_e)
                t_od = em.op("sy", lambda sy, k=k, j=j, half=half: sy.dma_start(
                    y_d[(k * 8 + j) * 128: (k * 8 + j) * 128 + 128,
                        half * 512:(half + 1) * 512],
                    otm[:, half * 512:(half + 1) * 512]), "od", 16)
                otm_half[half] = t_od
        st["x1_free"][par] = em.tok("pe")

    tap("kcol", kcol[:, :])
    WQ[0] = build_wq(0)
    pump(0, 10 ** 6)
    emit_rz(0)
    tap("un0", unb[:, :])
    tap("rows3", rows3[0:3, 32:1184])
    tap("zbd", zb[0][:, :])
    tap("wfmd", wfm[0][:, :])
    tap("x1r", x1[0][:, :])
    prefetch_w1(2)
    prefetch_w2(1)

    for k in range(NSUP):
        if k + 1 < NSUP:
            WQ[k + 1] = build_wq(k + 1)
        for e in range(NE):
            t_last = emit_expert(k, e, k + 1 if k + 1 < NSUP else None)
            if k == 0 and e == 0:
                tap("hidd", hid[:, :])
        st["experts_pe_done"][k] = em.tok("pe")
        if k == 0:
            tap("x1e", x1[0][:, :])
        if k + 1 < NSUP:
            pump(k + 1, 10 ** 6)
            emit_rz(k + 1)
        emit_outphase(k, t_last)

    em.op("sy", lambda sy: sy.wait_ge(em.sems["od"], em.cnt["od"]))

    with nc.Block() as block:
        @block.gpsimd
        def _(g):
            for fn in em.q["gp"]:
                fn(g)

        @block.tensor
        def _(t):
            for fn in em.q["pe"]:
                fn(t)

        @block.vector
        def _(v):
            for fn in em.q["ve"]:
                fn(v)

        @block.scalar
        def _(s):
            for fn in em.q["sc"]:
                fn(s)

        @block.sync
        def _(sy):
            for fn in em.q["sy"]:
                fn(sy)

    ctx.close()
    return nc


def _host_inputs(core_u, A, B, C, Dv, log_dt, ln_g, ln_b, rW, rb, W1, b1, W2, b2):
    f32 = np.float32
    bf = ml_dtypes.bfloat16
    return {
        "u": np.ascontiguousarray(core_u, dtype=f32),
        "Acol": np.ascontiguousarray(np.reshape(A, (64, 1)), dtype=f32),
        "Bmat": np.ascontiguousarray(B, dtype=f32),
        "Ccol": np.ascontiguousarray(
            np.reshape(C, (8, 128, 64)).transpose(1, 0, 2).reshape(128, 512),
            dtype=f32),
        "Dcol": np.ascontiguousarray(np.reshape(Dv, (8, 128)).T, dtype=f32),
        "ldtrow": np.ascontiguousarray(np.reshape(log_dt, (1, 1024)), dtype=f32),
        "gcol": np.ascontiguousarray(np.reshape(ln_g, (8, 128)).T, dtype=f32),
        "bcol": np.ascontiguousarray(np.reshape(ln_b, (8, 128)).T, dtype=f32),
        "rWp": np.ascontiguousarray(
            np.reshape(rW, (8, 128, 8)).transpose(1, 0, 2).reshape(128, 64),
            dtype=f32),
        "rbrow": np.ascontiguousarray(np.reshape(rb, (1, 8)), dtype=f32),
        "W1": np.ascontiguousarray(W1, dtype=f32),
        "b1col": np.ascontiguousarray(
            np.reshape(b1, (8, 16, 128)).transpose(2, 1, 0).reshape(128, 128),
            dtype=f32),
        "W2": np.ascontiguousarray(W2, dtype=f32),
        "b2sb": np.ascontiguousarray(b2, dtype=f32),
        "ident32": np.eye(128, dtype=f32),
        "identbf": np.eye(128, dtype=f32).astype(bf),
        "ones32": np.ones((128, 1), dtype=f32),
        "onesbf": np.ones((128, 1), dtype=f32).astype(bf),
        "ones1b": np.ones((1, 128), dtype=f32).astype(bf),
        "oneh8": np.repeat(np.eye(8, dtype=f32), 128, axis=1).astype(bf),
        "oneh3": np.repeat(np.eye(3, dtype=f32), 128, axis=1).astype(bf),
        "cvals": np.tile(np.array([[1e-5, -0.5]], dtype=f32), (128, 1)),
    }


def kernel(L0_attn_u, L0_attn_A, L0_attn_B, L0_attn_C, L0_attn_D, L0_attn_log_dt,
           L0_ffn_ln_g, L0_ffn_ln_b, L0_ffn_rW, L0_ffn_rb, L0_ffn_W1, L0_ffn_b1,
           L0_ffn_W2, L0_ffn_b2):
    u = np.asarray(L0_attn_u)
    args = tuple(np.asarray(a) for a in (
        L0_attn_A, L0_attn_B, L0_attn_C, L0_attn_D, L0_attn_log_dt,
        L0_ffn_ln_g, L0_ffn_ln_b, L0_ffn_rW, L0_ffn_rb, L0_ffn_W1, L0_ffn_b1,
        L0_ffn_W2, L0_ffn_b2))
    nc = build_nc()
    in_maps = [_host_inputs(u[c], *args) for c in range(8)]
    res = run_bass_kernel_spmd(nc, in_maps, core_ids=list(range(8)))
    return np.stack([np.asarray(r["y"], dtype=np.float32) for r in res.results],
                    axis=0)


# version 11
# speedup vs baseline: 1.0403x; 1.0403x over previous
import contextlib
import numpy as np
import ml_dtypes
import concourse.bass as bass
import concourse.mybir as mybir
from concourse.bass_utils import run_bass_kernel_spmd

F32 = mybir.dt.float32
BF16 = mybir.dt.bfloat16
AF = mybir.ActivationFunctionType
ALU = mybir.AluOpType
AX = mybir.AxisListType

D = 1024
T = 4096
S = 1024
NSUP = T // S
CCN = 8
FF = 2048
NFC = FF // 128
NE = 8
NMC = 8
LTAP = 64
WCOL = 1088
Q = 128
NQ = S // Q
EPS = 1e-5
NTILE = T // 128
TAPG = 4
OVERLAP_RZ = 1
_DBG = set()


class Emitter:
    QS = ("pe", "ve", "sc", "gp", "sy")

    def __init__(self):
        self.q = {k: [] for k in self.QS}
        self.cnt = {k: 0 for k in ("pe", "ve", "sc", "gd", "od")}
        self.wm = {}
        self.sems = {}

    def tok(self, sem):
        return (sem, self.cnt[sem])

    def wait(self, qn, token):
        if token is None:
            return
        sem, val = token
        if val <= 0:
            return
        key = (qn, sem)
        if self.wm.get(key, 0) >= val:
            return
        self.wm[key] = val
        h = self.sems[sem]
        self.q[qn].append(lambda eng, h=h, v=val: eng.wait_ge(h, v))

    def waits(self, qn, tokens):
        for t in tokens:
            self.wait(qn, t)

    def op(self, qn, fn, inc=None, amt=1):
        if inc is None:
            self.q[qn].append(fn)
            return None
        self.cnt[inc] += amt
        h = self.sems[inc]
        self.q[qn].append(lambda eng, h=h, a=amt, f=fn: f(eng).then_inc(h, a))
        return (inc, self.cnt[inc])

    def opw(self, qn, fn):
        t = self.op(qn, fn, qn)
        self.wait(qn, t)
        return t

    def fence(self, qn):
        self.cnt[qn] += 1
        h = self.sems[qn]
        self.q[qn].append(lambda eng, h=h: eng.nop().then_inc(h, 1))
        return (qn, self.cnt[qn])


def build_nc():
    nc = bass.Bass("TRN2", target_bir_lowering=False,
                   detect_race_conditions=False)
    em = Emitter()

    u_d = nc.dram_tensor("u", [T, D], F32, kind="ExternalInput").ap()
    A_d = nc.dram_tensor("Acol", [64, 1], F32, kind="ExternalInput").ap()
    B_d = nc.dram_tensor("Bmat", [64, D], F32, kind="ExternalInput").ap()
    C_d = nc.dram_tensor("Ccol", [128, 512], F32, kind="ExternalInput").ap()
    Dc_d = nc.dram_tensor("Dcol", [128, 8], F32, kind="ExternalInput").ap()
    ldt_d = nc.dram_tensor("ldtrow", [1, D], F32, kind="ExternalInput").ap()
    g_d = nc.dram_tensor("gcol", [128, 8], F32, kind="ExternalInput").ap()
    bb_d = nc.dram_tensor("bcol", [128, 8], F32, kind="ExternalInput").ap()
    rw_d = nc.dram_tensor("rWp", [128, 64], F32, kind="ExternalInput").ap()
    rb_d = nc.dram_tensor("rbrow", [1, 8], F32, kind="ExternalInput").ap()
    w1_d = nc.dram_tensor("W1", [NE, D, FF], F32, kind="ExternalInput").ap()
    b1_d = nc.dram_tensor("b1col", [128, 128], F32, kind="ExternalInput").ap()
    w2_d = nc.dram_tensor("W2", [NE, FF, D], F32, kind="ExternalInput").ap()
    b2_d = nc.dram_tensor("b2sb", [8, D], F32, kind="ExternalInput").ap()
    id32_d = nc.dram_tensor("ident32", [128, 128], F32, kind="ExternalInput").ap()
    idbf_d = nc.dram_tensor("identbf", [128, 128], BF16, kind="ExternalInput").ap()
    ones32_d = nc.dram_tensor("ones32", [128, 1], F32, kind="ExternalInput").ap()
    onesbf_d = nc.dram_tensor("onesbf", [128, 1], BF16, kind="ExternalInput").ap()
    ones1b_d = nc.dram_tensor("ones1b", [1, 128], BF16, kind="ExternalInput").ap()
    oh8_d = nc.dram_tensor("oneh8", [8, 1024], BF16, kind="ExternalInput").ap()
    oh3_d = nc.dram_tensor("oneh3", [3, 384], BF16, kind="ExternalInput").ap()
    cv_d = nc.dram_tensor("cvals", [128, 2], F32, kind="ExternalInput").ap()
    y_d = nc.dram_tensor("y", [T, D], F32, kind="ExternalOutput").ap()

    ctx = contextlib.ExitStack()
    al = ctx.enter_context

    def sbuf(name, shape, dt):
        return al(nc.sbuf_tensor(name, shape, dt))

    x1 = [sbuf("x1a", [128, CCN * S], F32), sbuf("x1b", [128, CCN * S], F32)]
    unb = sbuf("una", [128, CCN * WCOL], BF16)
    zb = [sbuf("zbufa", [128, CCN * S], BF16), sbuf("zbufb", [128, CCN * S], BF16)]
    hid = sbuf("hid", [128, NFC * S], BF16)
    w1r = sbuf("w1r", [128, 3 * 1024], BF16)
    w2r = sbuf("w2r", [128, 2 * 2048], BF16)
    wrep = sbuf("wrep", [128, S], BF16)
    utm = sbuf("utm", [128, 2 * 1024], BF16)
    otm = sbuf("otm", [128, 1024], F32)
    z0s = sbuf("z0s", [128, CCN * Q], F32)
    tz = sbuf("tz", [128, 128], F32)
    reps = sbuf("reps", [128, max(4 * Q, 512)], F32)
    rows3 = sbuf("rows3", [3, NTILE * 128 + 128], BF16)
    ucols = sbuf("ucols", [128, 32], F32)
    kcol = sbuf("kcol", [128, 512], F32)
    lg = sbuf("lg", [8, S], F32)
    wfm = [sbuf("wfma", [8, S], BF16), sbuf("wfmb", [8, S], BF16)]
    sft8 = sbuf("sft8", [128, 272], F32)
    zrow = sbuf("zrow", [1, 2 * Q], F32)
    zrow2 = sbuf("zrow2", [1, Q], F32)
    zrbf = sbuf("zrbf", [1, 2 * Q], BF16)
    rwp = sbuf("rwp", [128, 64], F32)
    rwp2 = sbuf("rwp2", [128, 64], F32)
    rbc = sbuf("rbc", [1, 8], F32)
    rbcol = sbuf("rbcol", [8, 1], F32)
    b2bf = sbuf("sb_b2bf", [8, D], BF16)
    b1col = sbuf("sb_b1col", [128, 128], F32)
    gcol = sbuf("sb_gcol", [128, 8], F32)
    bcol = sbuf("sb_bcol", [128, 8], F32)
    dcol = sbuf("sb_dcol", [128, 8], F32)
    acol = sbuf("sb_acol", [64, 1], F32)
    id32 = sbuf("sb_id32", [128, 128], F32)
    idbf = sbuf("sb_idbf", [128, 128], BF16)
    ones32 = sbuf("sb_ones32", [128, 1], F32)
    onesbf = sbuf("sb_onesbf", [128, 1], BF16)
    ones1b = sbuf("sb_ones1b", [1, 128], BF16)
    oh8 = sbuf("sb_oh8", [8, 1024], BF16)
    oh3 = sbuf("sb_oh3", [3, 384], BF16)
    ccol = sbuf("ccol", [64, 1], F32)
    dgb = sbuf("dgb", [128, 4 * 128], BF16)

    cval = sbuf("cval", [128, 2], F32)
    xsq8 = sbuf("xsq8", [128, 1024], BF16)
    smallr = x1[1][0:1, 0:1088]
    rept = reps[:, 2 * Q:2 * Q + 256]

    Cc = x1[1][0:128, 1152:1664]
    Bm = x1[1][0:64, 1664:2688]
    Psb = x1[1][0:64, 2688:3712]
    Abar = x1[1][0:64, 3712:4736]
    cur = x1[1][0:64, 4736:5760]
    cur2 = x1[1][0:64, 5760:6784]

    pW1 = al(nc.psum_tensor("pW1", [128, 1024], F32))
    pW2 = al(nc.psum_tensor("pW2", [128, 1024], F32))
    pT = al(nc.psum_tensor("pTp", [128, 2048], BF16))
    pM = al(nc.psum_tensor("pM", [128, 1024], F32))

    for sname in ("pe", "ve", "sc", "gd", "od",
                  "w1s0", "w1s1", "w1s2", "w2s0", "w2s1", "ut0", "ut1"):
        em.sems[sname] = al(nc.semaphore("s_" + sname))
        em.cnt[sname] = 0

    st = {
        "z0s_ve": None, "z0s_pe": None,
        "rept_ve": None, "reps01_ve": None, "reps23_ve": None,
        "stats_pe": None, "sft_ve": None,
        "pmA": None, "pmB": None,
        "pt": {0: None, 1: None},
        "pw1": {0: None, 1: None},
        "pw2": {0: None, 1: None},
        "gel": {0: None, 1: None},
        "utm": {0: [], 1: []},
        "rows3_sc": None,
        "ucols_pe": {0: None, 1: None},
        "zrbf_pe": None,
        "wrep_sc": None,
        "hid_free_pe": None,
        "experts_pe_done": {},
        "x1_free": {0: None, 1: None},
    }
    last_unb = {}
    conv_done = {}
    recon_done = {}
    zb_ready = {}
    wfm_ready = {}
    tile_dma = {}
    tile_slot = {}
    tile_stats_ve = {}
    stats_done = set()
    tile_seq = {"n": 0}
    pt_seq = {"n": 0}
    w1_slab_tok = []
    w2_slab_tok = []
    w1_gd = []
    w2_gd = []
    w1_emitted = {"n": 0}
    w2_emitted = {"n": 0}
    w1_consumed = {"n": 0}
    w2_consumed = {"n": 0}

    def dma(out_ap, in_ap, sem="gd"):
        return em.op("gp", lambda g, o=out_ap, i=in_ap: g.dma_start(o, i), sem, 16)

    def tap(name, src_ap):
        if name not in _DBG:
            return
        o = nc.dram_tensor("dbg_" + name, list(src_ap.shape), F32,
                           kind="ExternalOutput").ap()
        for s_ in ("pe", "ve", "sc"):
            em.wait("gp", em.tok(s_))
        tk = em.op("gp", lambda g, o=o, a=src_ap: g.dma_start(o, a), "gd", 16)
        for qn in ("pe", "ve", "sc", "sy"):
            em.wait(qn, tk)

    for dst, src in ((acol[:, :], A_d), (Bm, B_d), (Cc, C_d), (dcol[:, :], Dc_d),
                     (smallr[0:1, 64:1088], ldt_d), (gcol[:, :], g_d),
                     (bcol[:, :], bb_d), (rwp[:, :], rw_d), (rbc[:, :], rb_d),
                     (b1col[:, :], b1_d), (b2bf[:, :], b2_d),
                     (id32[:, :], id32_d), (idbf[:, :], idbf_d),
                     (ones32[:, :], ones32_d), (onesbf[:, :], onesbf_d),
                     (ones1b[:, :], ones1b_d), (oh8[:, :], oh8_d),
                     (oh3[:, :], oh3_d), (cval[:, :], cv_d)):
        CONST = dma(dst, src)
    nc.const_aps.register(F32, EPS, cval[:, 0:1])
    nc.const_aps.register(F32, -0.5, cval[:, 1:2])

    def k_setup():
        em.wait("pe", CONST)
        for cc in range(CCN):
            tk = em.op("pe", lambda t, cc=cc: t.matmul(
                pM[0:1, 0:64], ones32[:, :], Cc[:, cc * 64:(cc + 1) * 64],
                start=(cc == 0), stop=(cc == 7)), "pe" if cc == 7 else None)
        em.wait("sc", tk)
        em.opw("sc", lambda s: s.copy(smallr[0:1, 0:64], pM[0:1, 0:64]))
        t_crow = em.fence("sc")
        em.wait("pe", t_crow)
        t_cc = em.op("pe", lambda t: t.transpose(
            pM[0:64, 100:101], smallr[0:1, 0:64], id32[0:1, 0:1]), "pe")
        em.wait("sc", t_cc)
        em.opw("sc", lambda s: s.copy(ccol[:, :], pM[0:64, 100:101]))
        t_ccol = em.fence("sc")
        em.wait("sc", CONST)
        t_dt = em.op("sc", lambda s: s.activation(
            smallr[0:1, 64:1088], smallr[0:1, 64:1088], AF.Exp), "sc")
        em.wait("ve", CONST)
        t_o = em.op("ve", lambda v: v.memset(zrow[0:1, 0:64], 1.0), "ve")
        em.waits("pe", [t_dt, t_o, t_ccol])
        em.op("pe", lambda t: t.matmul(
            pM[0:64, 0:512], zrow[0:1, 0:64], smallr[0:1, 64:576],
            start=True, stop=True))
        t_b2 = em.op("pe", lambda t: t.matmul(
            pM[0:64, 512:1024], zrow[0:1, 0:64], smallr[0:1, 576:1088],
            start=True, stop=True), "pe")
        em.wait("ve", t_b2)
        t_p = em.op("ve", lambda v: v.tensor_scalar(
            Psb, pM[0:64, 0:1024], acol[:, 0:1], None, ALU.mult), "ve")
        em.wait("sc", t_p)
        t_ab = em.op("sc", lambda s: s.activation(Abar, Psb, AF.Exp), "sc")
        t_cur = em.op("ve", lambda v: v.tensor_scalar(
            cur, Bm, ccol[:, 0:1], None, ALU.mult), "ve")
        em.waits("pe", [t_ab, t_cur])
        em.wait("ve", t_ab)
        bufs = [cur, cur2]
        tok_ve = {0: t_cur, 1: None}
        tok_pe = {0: None, 1: None}
        for l in range(LTAP):
            b = l % 2
            em.wait("pe", tok_ve[b])
            for cc in range(CCN):
                tk = em.op("pe", lambda t, cc=cc, l=l, b=b: t.matmul(
                    pW1[:, cc * 64 + l: cc * 64 + l + 1],
                    bufs[b][:, cc * 128:(cc + 1) * 128], ones32[0:64, 0:1],
                    start=True, stop=True), "pe" if cc == 7 else None)
            tok_pe[b] = tk
            if l < LTAP - 1:
                nb = 1 - b
                em.wait("ve", tok_pe[nb])
                tok_ve[nb] = em.op("ve", lambda v, b=b, nb=nb: v.tensor_tensor(
                    bufs[nb], bufs[b], Abar, ALU.mult), "ve")
        em.wait("ve", tk)
        for cc in range(CCN):
            t_df = em.op("ve", lambda v, cc=cc: v.tensor_tensor(
                pW1[:, cc * 64 + 31: cc * 64 + 32],
                pW1[:, cc * 64 + 31: cc * 64 + 32],
                dcol[:, cc:cc + 1], ALU.add), "ve" if cc == 7 else None)
        em.wait("sc", t_df)
        t_k = em.op("sc", lambda s: s.copy(kcol[:, :], pW1[:, 0:512]), "sc")
        em.wait("ve", t_k)
        st["pw1"][0] = t_k
        st["pw1"][1] = t_k

        for cc in range(CCN):
            t_rw = em.op("ve", lambda v, cc=cc: v.tensor_scalar(
                rwp2[:, cc * 8:(cc + 1) * 8], rwp[:, cc * 8:(cc + 1) * 8],
                gcol[:, cc:cc + 1], None, ALU.mult), "ve" if cc == 7 else None)
        for cc in range(CCN):
            t_rb = em.op("pe", lambda t, cc=cc: t.matmul(
                pM[0:1, 200:208], bcol[:, cc:cc + 1], rwp[:, cc * 8:(cc + 1) * 8],
                start=(cc == 0), stop=(cc == 7)), "pe" if cc == 7 else None)
        em.wait("ve", t_rb)
        em.opw("ve", lambda v: v.tensor_tensor(
            rbc[:, :], pM[0:1, 200:208], rbc[:, :], ALU.add))
        t_rb2 = em.fence("ve")
        em.wait("pe", t_rb2)
        t_rb3 = em.op("pe", lambda t: t.transpose(
            pM[0:8, 300:301], rbc[0:1, 0:8], id32[0:1, 0:1]), "pe")
        em.wait("sc", t_rb3)
        em.opw("sc", lambda s: s.copy(rbcol[:, :], pM[0:8, 300:301]))
        t_rb4 = em.fence("sc")
        st["pmA"] = t_rb4
        st["pmB"] = t_rb4
        st["hid_free_pe"] = em.tok("pe")
        _ = t_rw

    k_setup()

    def win_tiles(w):
        return [i for i in range(8 * w - 1, 8 * w + 9) if 0 <= i < NTILE]

    def emit_tile_dma(w, i):
        slot = tile_seq["n"] % 2
        tile_seq["n"] += 1
        tile_slot[(w, i)] = slot
        for t in st["utm"][slot]:
            em.wait("gp", t)
        tile_dma[(w, i)] = dma(utm[:, slot * 1024:(slot + 1) * 1024],
                               u_d[i * 128:(i + 1) * 128, :], "ut%d" % slot)

    stats_seq = {"n": 0}
    stats_tok = {}

    def emit_tile_stats_ve(w, i):
        slot = tile_slot[(w, i)]
        sl = stats_seq["n"] % 2
        stats_seq["n"] += 1
        us = utm[:, slot * 1024:(slot + 1) * 1024]
        uc = ucols[:, sl * 16:(sl + 1) * 16]
        em.wait("ve", tile_dma[(w, i)])
        em.wait("ve", st["ucols_pe"][sl])
        em.wait("ve", st.get("z0s_pe"))
        em.opw("ve", lambda v, uc=uc, us=us: v.reduce_sum(
            uc[:, 0:1], us, axis=AX.X))
        t_sq = em.op("ve", lambda v, us=us: v.tensor_tensor(
            z0s[:, 0:1024], us, us, ALU.mult), "ve")
        tile_stats_ve[(w, i)] = (t_sq,)
        em.opw("ve", lambda v, uc=uc: v.reduce_sum(
            uc[:, 1:2], z0s[:, 0:1024], axis=AX.X))
        em.opw("ve", lambda v, uc=uc: v.tensor_scalar(
            uc[:, 8:9], uc[:, 0:1], 1.0 / D, None, ALU.mult))
        em.opw("ve", lambda v, uc=uc: v.tensor_tensor(
            uc[:, 3:4], uc[:, 8:9], uc[:, 8:9], ALU.mult))
        em.opw("ve", lambda v, uc=uc: v.scalar_tensor_tensor(
            uc[:, 4:5], uc[:, 1:2], 1.0 / D, uc[:, 3:4],
            ALU.mult, ALU.subtract))
        em.opw("ve", lambda v, uc=uc: v.tensor_scalar(
            uc[:, 4:5], uc[:, 4:5], EPS, None, ALU.add))
        em.opw("ve", lambda v, uc=uc: v.reciprocal(uc[:, 5:6], uc[:, 4:5]))
        for _ in range(5):
            em.opw("ve", lambda v, uc=uc: v.tensor_tensor(
                uc[:, 6:7], uc[:, 5:6], uc[:, 5:6], ALU.mult))
            em.opw("ve", lambda v, uc=uc: v.tensor_tensor(
                uc[:, 6:7], uc[:, 6:7], uc[:, 4:5], ALU.mult))
            em.opw("ve", lambda v, uc=uc: v.tensor_scalar(
                uc[:, 6:7], uc[:, 6:7], -0.5, 1.5, ALU.mult, ALU.add))
            em.opw("ve", lambda v, uc=uc: v.tensor_tensor(
                uc[:, 5:6], uc[:, 5:6], uc[:, 6:7], ALU.mult))
        em.opw("ve", lambda v, uc=uc: v.tensor_tensor(
            uc[:, 9:10], uc[:, 4:5], uc[:, 5:6], ALU.mult))
        em.opw("ve", lambda v, uc=uc: v.tensor_copy(uc[:, 10:11], uc[:, 5:6]))
        t_uc = em.fence("ve")
        stats_tok[(w, i)] = (t_uc, sl)

    def emit_tile_stats_pe(w, i):
        t_uc, sl = stats_tok.pop((w, i))
        uc = ucols[:, sl * 16:(sl + 1) * 16]
        em.waits("pe", [t_uc, st["pmA"]])
        t_r = em.op("pe", lambda t, uc=uc: t.matmul(
            pM[0:3, 0:128], uc[:, 8:11], id32[:, :], start=True, stop=True), "pe")
        st["ucols_pe"][sl] = t_r
        em.wait("sc", t_r)
        t_s = em.op("sc", lambda s, i=i: s.copy(
            rows3[0:3, 32 + i * 128: 32 + (i + 1) * 128], pM[0:3, 0:128]), "sc")
        st["rows3_sc"] = t_s
        st["pmA"] = t_s
        stats_done.add(i)

    def emit_tile_tp_un(w, i):
        slot = tile_slot[(w, i)]
        half = pt_seq["n"] % 2
        pt_seq["n"] += 1
        us = utm[:, slot * 1024:(slot + 1) * 1024]
        a = max(i * 128, w * S - 32)
        b = min(i * 128 + 128, w * S + S + 32)
        sj0 = a - i * 128
        wd0 = a - (w * S - 32)
        n = b - a
        em.waits("pe", [st["pt"][half], tile_dma[(w, i)]])
        for cc in range(CCN):
            t_tp = em.op("pe", lambda t, cc=cc, half=half, us=us: t.transpose(
                pT[:, half * 1024 + cc * 128: half * 1024 + (cc + 1) * 128],
                us[:, cc * 128:(cc + 1) * 128], idbf[:, :]),
                "pe" if cc == 7 else None)
        st["utm"][slot] = [t_tp]
        tsv = tile_stats_ve.get((w, i))
        if tsv is not None:
            st["utm"][slot].extend(tsv)
        em.waits("pe", [st["rows3_sc"], st["pmB"]])
        rc = 32 + a
        em.op("pe", lambda t, rc=rc, n=n: t.matmul(
            pM[:, 512:512 + n], oh3[0:3, 0:128], rows3[0:3, rc:rc + n],
            start=True, stop=True))
        t_bc = em.op("pe", lambda t, rc=rc, n=n: t.matmul(
            pM[:, 768:768 + n], oh3[0:3, 256:384], rows3[0:3, rc:rc + n],
            start=True, stop=True), "pe")
        em.waits("sc", [t_bc, st.get("rept_ve")])
        em.op("sc", lambda s, n=n: s.copy(rept[:, 0:n], pM[:, 512:512 + n]))
        t_rp = em.op("sc", lambda s, n=n: s.copy(
            rept[:, 128:128 + n], pM[:, 768:768 + n]), "sc")
        st["pmB"] = t_rp
        em.waits("ve", [t_tp, t_rp])
        em.wait("ve", conv_done.get(w - 1))
        for cc in range(CCN):
            em.op("ve", lambda v, cc=cc, half=half, sj0=sj0, n=n: v.tensor_tensor(
                tz[:, 0:n],
                pT[:, half * 1024 + cc * 128 + sj0: half * 1024 + cc * 128 + sj0 + n],
                rept[:, 0:n], ALU.subtract))
            t_un = em.op("ve", lambda v, cc=cc, wd0=wd0, n=n: v.tensor_tensor(
                unb[:, cc * WCOL + wd0: cc * WCOL + wd0 + n],
                tz[:, 0:n], rept[:, 128:128 + n], ALU.mult),
                "ve" if cc == 7 else None)
        st["pt"][half] = t_un
        st["rept_ve"] = t_un
        st["reps23_ve"] = t_un
        last_unb[w] = t_un

    def emit_pads(w):
        if w == 0:
            for cc in range(CCN):
                t_p = em.op("ve", lambda v, cc=cc: v.memset(
                    unb[:, cc * WCOL: cc * WCOL + 32], 0.0),
                    "ve" if cc == 7 else None)
            last_unb[0] = t_p
        if w == NSUP - 1:
            em.wait("ve", conv_done.get(w - 1))
            for cc in range(CCN):
                t_p = em.op("ve", lambda v, cc=cc: v.memset(
                    unb[:, cc * WCOL + 1056: cc * WCOL + 1088], 0.0),
                    "ve" if cc == 7 else None)
            last_unb[w] = t_p

    dg_pe = [None, None, None, None]

    def emit_conv_pe(w):
        par = w % 2
        em.wait("pe", last_unb[w])
        em.wait("ve", st["x1_free"][par])
        nd = {"n": 0}
        for cc in range(CCN):
            em.waits("pe", [st["pw1"][0], st["pw1"][1]])
            for l in range(LTAP):
                slot = nd["n"] % 4
                nd["n"] += 1
                em.wait("ve", dg_pe[slot])
                kb = kcol[:, cc * 64 + l: cc * 64 + l + 1].to_broadcast([128, 128])
                t_d = em.op("ve", lambda v, slot=slot, kb=kb: v.tensor_tensor(
                    dgb[:, slot * 128:(slot + 1) * 128], idbf[:, :], kb, ALU.mult),
                    "ve")
                em.wait("pe", t_d)
                for half in range(2):
                    src = unb[:, cc * WCOL + (63 - l) + half * 512:
                              cc * WCOL + (63 - l) + half * 512 + 512]
                    t_mm = em.op("pe", lambda t, slot=slot, src=src, half=half,
                                 st_=(l == 0), sp_=(l == LTAP - 1):
                                 t.matmul(pW1[:, half * 512:(half + 1) * 512],
                                          dgb[:, slot * 128:(slot + 1) * 128], src,
                                          start=st_, stop=sp_),
                                 "pe" if half == 1 else None)
                dg_pe[slot] = t_mm
            em.wait("ve", t_mm)
            for half in range(2):
                t_ev = em.op("ve", lambda v, cc=cc, half=half, par=par:
                             v.tensor_copy(
                                 x1[par][:, cc * S + half * 512:
                                         cc * S + half * 512 + 512],
                                 pW1[:, half * 512:(half + 1) * 512]), "ve")
            st["pw1"][0] = t_ev
            st["pw1"][1] = t_ev
        conv_done[w] = t_ev

    def emit_conv_group(w, cc, g):
        par = w % 2
        if cc == 0 and g == 0:
            em.wait("ve", st["x1_free"][par])
            em.wait("ve", last_unb[w])
            em.wait("ve", st["hid_free_pe"])
        for l in range(g * TAPG, (g + 1) * TAPG):
            src = unb[:, cc * WCOL + (63 - l): cc * WCOL + (63 - l) + S]
            dst = x1[par][:, cc * S:(cc + 1) * S]
            sc_ap = kcol[:, cc * 64 + l: cc * 64 + l + 1]
            last = (l == LTAP - 1)
            if l == 0:
                em.op("ve", lambda v, src=src, dst=dst, sc_ap=sc_ap: v.tensor_scalar(
                    dst, src, sc_ap, None, ALU.mult))
            else:
                tk = em.op("ve", lambda v, src=src, dst=dst, sc_ap=sc_ap:
                           v.scalar_tensor_tensor(dst, src, sc_ap, dst,
                                                  ALU.mult, ALU.add),
                           "ve" if last else None)
                if last and cc == CCN - 1:
                    conv_done[w] = tk

    recon_prev = {}

    def emit_recon_q(w, qi):
        par = w % 2
        t0 = qi * Q
        rc = 32 + w * S + t0
        em.waits("pe", [st["pmA"], st["rows3_sc"]])
        em.op("pe", lambda t, rc=rc: t.matmul(
            pM[:, 0:Q], oh3[0:3, 0:128], rows3[0:3, rc:rc + Q],
            start=True, stop=True))
        t_b = em.op("pe", lambda t, rc=rc: t.matmul(
            pM[:, Q:2 * Q], oh3[0:3, 128:256], rows3[0:3, rc:rc + Q],
            start=True, stop=True), "pe")
        em.waits("sc", [t_b, recon_prev.get(w), st.get("reps01_ve")])
        em.op("sc", lambda s: s.copy(reps[:, 0:Q], pM[:, 0:Q]))
        t_e = em.op("sc", lambda s: s.copy(reps[:, Q:2 * Q], pM[:, Q:2 * Q]), "sc")
        st["pmA"] = t_e
        em.waits("ve", [t_e, conv_done[w]])
        un3 = unb[:, :].rearrange("p (c t) -> p c t", c=8)[
            :, :, 32 + t0: 32 + t0 + Q]
        xq3 = x1[par][:, :].rearrange("p (c t) -> p c t", c=8)[
            :, :, t0:t0 + Q]
        iv_b = reps[:, Q:2 * Q].unsqueeze(1).to_broadcast([128, 8, Q])
        mu_b = reps[:, 0:Q].unsqueeze(1).to_broadcast([128, 8, Q])
        tz3 = z0s[:, :].rearrange("p (c t) -> p c t", c=8)
        em.op("ve", lambda v, un3=un3, iv_b=iv_b: v.tensor_tensor(
            tz3, un3, iv_b, ALU.mult))
        em.op("ve", lambda v, xq3=xq3: v.tensor_tensor(
            xq3, xq3, tz3, ALU.add))
        t_r = em.op("ve", lambda v, xq3=xq3, mu_b=mu_b: v.tensor_tensor(
            xq3, xq3, mu_b, ALU.add), "ve")
        recon_prev[w] = t_r
        if qi == NQ - 1:
            recon_done[w] = t_r
            st["reps01_ve"] = t_r

    WQ = {w: [] for w in range(NSUP)}
    WQ_pace = {}
    WQ_rest = {}

    def build_wq(w):
        tasks = []
        fin = []

        def finish_one():
            j, is_new = fin.pop(0)
            if is_new:
                tasks.append(lambda w=w, j=j: emit_tile_stats_pe(w, j))
            tasks.append(lambda w=w, j=j: emit_tile_tp_un(w, j))

        for i in win_tiles(w):
            tasks.append(lambda w=w, i=i: emit_tile_dma(w, i))
            if i not in stats_done:
                tasks.append(lambda w=w, i=i: emit_tile_stats_ve(w, i))
                fin.append((i, True))
            else:
                fin.append((i, False))
            while len(fin) > 1:
                finish_one()
        while fin:
            finish_one()
        tasks.append(lambda w=w: emit_pads(w))
        if w == 0:
            tasks.append(lambda w=w: emit_conv_pe(w))
        else:
            for cc in range(CCN):
                for g in range(LTAP // TAPG):
                    tasks.append(lambda w=w, cc=cc, g=g: emit_conv_group(w, cc, g))
        zts = zphase_tasks(w)
        if OVERLAP_RZ >= 1:
            for qi in range(NQ):
                tasks.append(lambda w=w, qi=qi: emit_recon_q(w, qi))
            ncut = {1: 0, 2: 20, 3: len(zts) - 3, 4: len(zts),
                    5: 4, 6: 1}[OVERLAP_RZ]
            tasks.extend(zts[:ncut])
            WQ_rest[w] = zts[ncut:]
        else:
            WQ_rest[w] = ([lambda w=w, qi=qi: emit_recon_q(w, qi)
                           for qi in range(NQ)] + zts)
        return tasks

    def emit_rz(w):
        for fn in WQ_rest.get(w, []):
            fn()

    PUMP_SLOTS = NE * 48

    def pump(w, n):
        if w is None or w >= NSUP:
            return
        q = WQ[w]
        if n >= 10 ** 5:
            while q:
                q.pop(0)()
            return
        pc = WQ_pace.setdefault(w, {"credit": 0.0, "rate": None})
        if pc["rate"] is None:
            pc["rate"] = (len(q) + 8) / float(PUMP_SLOTS)
        pc["credit"] += n * pc["rate"]
        while pc["credit"] >= 1.0 and q:
            q.pop(0)()
            pc["credit"] -= 1.0

    zst = {}

    def stats_pass(base_ap, cstride, qs, src3, t_prev):
        em.waits("sc", [st["stats_pe"], t_prev])
        t_c = em.op("sc", lambda s, src3=src3: s.activation(
            xsq8[:, :].rearrange("p (c t) -> p c t", c=8), src3,
            AF.Square), "sc")
        em.waits("pe", [t_c, st["pmA"], st["pmB"]])
        for cc in range(CCN):
            em.op("pe", lambda t, cc=cc: t.matmul(
                pM[0:1, 0:Q], ones32[:, :],
                base_ap[:, cc * cstride + qs: cc * cstride + qs + Q],
                start=(cc == 0), stop=(cc == 7)))
            t_m = em.op("pe", lambda t, cc=cc: t.matmul(
                pM[0:1, 512:512 + Q], onesbf[:, :], xsq8[:, cc * Q:(cc + 1) * Q],
                start=(cc == 0), stop=(cc == 7)), "pe")
        st["stats_pe"] = t_m
        return t_m

    def rows_from_stats(t_m):
        em.wait("sc", t_m)
        em.op("sc", lambda s: s.copy(zrow[0:1, 0:Q], pM[0:1, 0:Q]))
        em.op("sc", lambda s: s.copy(zrow[0:1, Q:2 * Q], pM[0:1, 512:512 + Q]))
        em.op("sc", lambda s: s.mul(zrow[0:1, 0:Q], zrow[0:1, 0:Q], 1.0 / D))
        em.op("sc", lambda s: s.activation(
            zrow2[0:1, 0:Q], zrow[0:1, 0:Q], AF.Square), "sc")
        t_r = em.fence("sc")
        em.wait("ve", t_r)
        em.op("ve", lambda v: v.scalar_tensor_tensor(
            zrow[0:1, Q:2 * Q], zrow[0:1, Q:2 * Q], 1.0 / D, zrow2[0:1, 0:Q],
            ALU.mult, ALU.subtract), "ve")
        t_v = em.fence("ve")
        em.wait("sc", t_v)
        em.op("sc", lambda s: s.activation(
            zrow2[0:1, 0:Q], zrow[0:1, Q:2 * Q], AF.Ln, bias=EPS))
        em.op("sc", lambda s: s.activation(
            zrow[0:1, Q:2 * Q], zrow2[0:1, 0:Q], AF.Exp, scale=-0.5))
        em.wait("sc", st["zrbf_pe"])
        em.op("sc", lambda s: s.copy(
            zrbf[0:1, 0:2 * Q], zrow[0:1, 0:2 * Q]), "sc")
        return em.fence("sc")

    def _zx3(k):
        return x1[k % 2][:, :].rearrange("p (c t) -> p c t", c=8)

    _z3g = None

    def z_ln2_stats(k, qi):
        qs = qi * Q
        xq = _zx3(k)[:, :, qs:qs + Q]
        zst["xq"] = xq
        t_prev = zst.get("t_zw") if qi > 0 else recon_done[k]
        zst["t_m"] = stats_pass(x1[k % 2], S, qs, xq, t_prev)

    def z_ln2_apply(k, qi):
        z3 = z0s[:, :].rearrange("p (c t) -> p c t", c=8)
        st["pmA"] = ("sc", em.cnt["sc"] + 2)
        st["pmB"] = ("sc", em.cnt["sc"] + 2)
        t_zb = rows_from_stats(zst["t_m"])
        em.waits("pe", [t_zb, st["pmA"]])
        em.op("pe", lambda t: t.matmul(
            pM[:, 0:Q], ones1b[0:1, 0:128], zrbf[0:1, 0:Q],
            start=True, stop=True))
        t_b = em.op("pe", lambda t: t.matmul(
            pM[:, Q:2 * Q], ones1b[0:1, 0:128], zrbf[0:1, Q:2 * Q],
            start=True, stop=True), "pe")
        st["zrbf_pe"] = t_b
        em.waits("sc", [t_b, st["reps01_ve"]])
        t_e = em.op("sc", lambda s: s.copy(reps[:, 0:2 * Q], pM[:, 0:2 * Q]), "sc")
        st["pmA"] = t_e
        xq = zst["xq"]
        em.waits("ve", [t_e, zst.get("t_rt")])
        mu_b = reps[:, 0:Q].unsqueeze(1).to_broadcast([128, 8, Q])
        s_b = reps[:, Q:2 * Q].unsqueeze(1).to_broadcast([128, 8, Q])
        em.op("ve", lambda v, xq=xq, mu_b=mu_b, z3=z3: v.tensor_tensor(
            z3, xq, mu_b, ALU.subtract))
        em.op("ve", lambda v, s_b=s_b, z3=z3: v.tensor_tensor(
            z3, z3, s_b, ALU.mult))
        t_z0 = em.op("ve", lambda v, xq=xq, z3=z3: v.tensor_tensor(
            xq, xq, z3, ALU.add), "ve")
        st["reps01_ve"] = t_z0
        zst["t_z0"] = t_z0

    def z_ln3_stats(k, qi):
        z3 = z0s[:, :].rearrange("p (c t) -> p c t", c=8)
        zst["t_m"] = stats_pass(z0s, Q, 0, z3, zst["t_z0"])

    def z_ln3_apply(k, qi):
        z3 = z0s[:, :].rearrange("p (c t) -> p c t", c=8)
        st["pmA"] = ("sc", em.cnt["sc"] + 2)
        st["pmB"] = ("sc", em.cnt["sc"] + 2)
        t_zb = rows_from_stats(zst["t_m"])
        em.waits("pe", [t_zb, st["pmA"], st["pmB"]])
        em.op("pe", lambda t: t.matmul(
            pM[:, 0:Q], ones1b[0:1, 0:128], zrbf[0:1, 0:Q],
            start=True, stop=True))
        t_b = em.op("pe", lambda t: t.matmul(
            pM[:, Q:2 * Q], ones1b[0:1, 0:128], zrbf[0:1, Q:2 * Q],
            start=True, stop=True), "pe")
        st["zrbf_pe"] = t_b
        em.waits("sc", [t_b, st["reps23_ve"]])
        t_e = em.op("sc", lambda s: s.copy(
            reps[:, 2 * Q:4 * Q], pM[:, 0:2 * Q]), "sc")
        st["pmA"] = t_e
        em.wait("ve", t_e)
        mu3_b = reps[:, 2 * Q:3 * Q].unsqueeze(1).to_broadcast([128, 8, Q])
        s3_b = reps[:, 3 * Q:4 * Q].unsqueeze(1).to_broadcast([128, 8, Q])
        em.op("ve", lambda v, mu3_b=mu3_b, z3=z3: v.tensor_tensor(
            z3, z3, mu3_b, ALU.subtract))
        t_n = em.op("ve", lambda v, s3_b=s3_b, z3=z3: v.tensor_tensor(
            z3, z3, s3_b, ALU.mult), "ve")
        st["reps23_ve"] = t_n
        zst["t_n"] = t_n

    def z_route(k, qi):
        par = k % 2
        qs = qi * Q
        z3 = z0s[:, :].rearrange("p (c t) -> p c t", c=8)
        zb3 = zb[par][:, :].rearrange("p (c t) -> p c t", c=8)
        zbq = zb3[:, :, qs:qs + Q]
        g_b = gcol[:, :].unsqueeze(2).to_broadcast([128, 8, Q])
        b_b = bcol[:, :].unsqueeze(2).to_broadcast([128, 8, Q])
        em.wait("ve", st["experts_pe_done"].get(k - 2))
        em.op("ve", lambda v, zbq=zbq, g_b=g_b, z3=z3: v.tensor_tensor(
            zbq, z3, g_b, ALU.mult))
        t_zw = em.op("ve", lambda v, zbq=zbq, b_b=b_b: v.tensor_tensor(
            zbq, zbq, b_b, ALU.add), "ve")
        zst["t_zw"] = t_zw
        if qi == NQ - 1:
            zb_ready[k] = t_zw
        em.wait("pe", zst["t_n"])
        em.wait("pe", st["pmA"])
        for cc in range(CCN):
            t_rt = em.op("pe", lambda t, cc=cc: t.matmul(
                pM[0:8, 0:Q], rwp2[:, cc * 8:(cc + 1) * 8],
                z0s[:, cc * Q:(cc + 1) * Q],
                start=(cc == 0), stop=(cc == 7)), "pe" if cc == 7 else None)
        zst["t_rt"] = t_rt
        em.wait("ve", t_rt)
        t_lg = em.op("ve", lambda v, qs=qs: v.tensor_scalar(
            lg[0:8, qs:qs + Q], pM[0:8, 0:Q], rbcol[0:8, 0:1], None, ALU.add),
            "ve")
        st["pmA"] = t_lg
        zst["t_lg"] = t_lg

    def z_soft1(k):
        em.waits("pe", [zst["t_lg"], st["pmB"]])
        for ti in range(8):
            t_tp = em.op("pe", lambda t, ti=ti: t.transpose(
                pM[:, 512 + 8 * ti: 520 + 8 * ti], lg[0:8, ti * 128:(ti + 1) * 128],
                id32[0:8, 0:8]), "pe" if ti == 7 else None)
        em.waits("sc", [t_tp, st["sft_ve"]])
        em.opw("sc", lambda s: s.copy(sft8[:, 0:64], pM[:, 512:576]))
        t_lt = em.fence("sc")
        st["pmB"] = t_lt
        em.wait("ve", t_lt)
        s8 = sft8[:, 0:64].rearrange("p (i e) -> p i e", i=8)
        m8 = sft8[:, 256:264]
        em.opw("ve", lambda v, s8=s8, m8=m8: v.reduce_max(
            m8.unsqueeze(2), s8, axis=AX.X, negate=True))
        em.opw("ve", lambda v, s8=s8, m8=m8: v.tensor_tensor(
            s8, s8, m8.unsqueeze(2).to_broadcast([128, 8, 8]), ALU.add))
        zst["t_sm"] = em.fence("ve")

    def z_soft2(k):
        s8 = sft8[:, 0:64].rearrange("p (i e) -> p i e", i=8)
        e8 = sft8[:, 64:128].rearrange("p (i e) -> p i e", i=8)
        p8 = sft8[:, 128:192].rearrange("p (i e) -> p i e", i=8)
        k8 = sft8[:, 192:256].rearrange("p (i e) -> p i e", i=8)
        m8 = sft8[:, 256:264]
        c8 = sft8[:, 264:272]
        em.wait("sc", zst["t_sm"])
        em.opw("sc", lambda s: s.activation(e8, s8, AF.Exp))
        t_ex = em.fence("sc")
        em.wait("ve", t_ex)
        em.opw("ve", lambda v: v.reduce_sum(c8.unsqueeze(2), e8, axis=AX.X))
        em.opw("ve", lambda v: v.reciprocal(m8, c8))
        em.opw("ve", lambda v: v.tensor_tensor(
            p8, e8, m8.unsqueeze(2).to_broadcast([128, 8, 8]), ALU.mult))
        em.opw("ve", lambda v: v.reduce_max(
            m8.unsqueeze(2), p8, axis=AX.X, negate=True))
        em.opw("ve", lambda v: v.tensor_tensor(
            k8, p8, m8.unsqueeze(2).to_broadcast([128, 8, 8]), ALU.add))
        em.opw("ve", lambda v: v.tensor_scalar(
            k8, k8, 0.0, None, ALU.is_ge))
        em.opw("ve", lambda v: v.scalar_tensor_tensor(
            k8, k8, -1e30, p8, ALU.mult, ALU.add))
        em.opw("ve", lambda v: v.reduce_max(
            m8.unsqueeze(2), k8, axis=AX.X, negate=True))
        em.opw("ve", lambda v: v.tensor_tensor(
            k8, p8, m8.unsqueeze(2).to_broadcast([128, 8, 8]), ALU.add))
        em.opw("ve", lambda v: v.tensor_scalar(
            k8, k8, 0.0, None, ALU.is_ge))
        em.opw("ve", lambda v: v.tensor_tensor(k8, k8, p8, ALU.mult))
        em.opw("ve", lambda v: v.reduce_sum(c8.unsqueeze(2), k8, axis=AX.X))
        em.opw("ve", lambda v: v.tensor_scalar(c8, c8, 1e-8, None, ALU.max))
        em.opw("ve", lambda v: v.reciprocal(m8, c8))
        em.opw("ve", lambda v: v.tensor_tensor(
            k8, k8, m8.unsqueeze(2).to_broadcast([128, 8, 8]), ALU.mult))
        t_w = em.fence("ve")
        st["sft_ve"] = t_w
        zst["t_w"] = t_w

    def z_soft3(k):
        par = k % 2
        for ti in range(8):
            em.waits("pe", [zst["t_w"], st["pmB"]])
            t_tw = em.op("pe", lambda t, ti=ti: t.transpose(
                pM[0:8, 512:640], sft8[:, 192 + 8 * ti: 200 + 8 * ti], id32[:, :]),
                "pe")
            em.wait("sc", t_tw)
            em.wait("sc", st["experts_pe_done"].get(k - 2))
            t_wf = em.op("sc", lambda s, ti=ti, par=par: s.copy(
                wfm[par][0:8, ti * 128:(ti + 1) * 128], pM[0:8, 512:640]), "sc")
            st["pmB"] = t_wf
        wfm_ready[k] = t_wf
        st["z0s_pe"] = em.tok("pe")
        st["rept_ve"] = em.tok("ve")
        st["reps01_ve"] = em.tok("ve")
        st["reps23_ve"] = em.tok("ve")

    def zphase_tasks(k):
        ts_ = []
        for qi in range(NQ):
            ts_.append(lambda k=k, qi=qi: z_ln2_stats(k, qi))
            ts_.append(lambda k=k, qi=qi: z_ln2_apply(k, qi))
            ts_.append(lambda k=k, qi=qi: z_ln3_stats(k, qi))
            ts_.append(lambda k=k, qi=qi: z_ln3_apply(k, qi))
            ts_.append(lambda k=k, qi=qi: z_route(k, qi))
        ts_.append(lambda k=k: z_soft1(k))
        ts_.append(lambda k=k: z_soft2(k))
        ts_.append(lambda k=k: z_soft3(k))
        return ts_

    W1TOT = NSUP * NE * NFC
    W2TOT = NSUP * NE * NMC

    def prefetch_w1(upto):
        while w1_emitted["n"] <= min(upto, W1TOT - 1):
            n = w1_emitted["n"]
            slot = n % 3
            if n >= 3:
                em.wait("gp", w1_slab_tok[n - 3])
            e = (n // NFC) % NE
            fc = n % NFC
            src = w1_d.rearrange("e (hc p) f -> e p hc f", p=128)[
                e, :, :, fc * 128:(fc + 1) * 128]
            w1_gd.append(dma(w1r[:, slot * 1024:(slot + 1) * 1024], src,
                             "w1s%d" % slot))
            w1_emitted["n"] += 1

    def prefetch_w2(upto):
        while w2_emitted["n"] <= min(upto, W2TOT - 1):
            n = w2_emitted["n"]
            slot = n % 2
            if n >= 2:
                em.wait("gp", w2_slab_tok[n - 2])
            e = (n // NMC) % NE
            mc = n % NMC
            src = w2_d.rearrange("e (fc p) m -> e p fc m", p=128)[
                e, :, :, mc * 128:(mc + 1) * 128]
            w2_gd.append(dma(w2r[:, slot * 2048:(slot + 1) * 2048], src,
                             "w2s%d" % slot))
            w2_emitted["n"] += 1

    def emit_expert(k, e, pumpw):
        par = k % 2
        em.waits("pe", [wfm_ready[k], st["pw2"][0], st["pw2"][1]])
        em.op("pe", lambda t, e=e, par=par: t.matmul(
            pW2[:, 0:512], oh8[0:8, e * 128:(e + 1) * 128], wfm[par][0:8, 0:512],
            start=True, stop=True))
        t_wr = em.op("pe", lambda t, e=e, par=par: t.matmul(
            pW2[:, 512:1024], oh8[0:8, e * 128:(e + 1) * 128],
            wfm[par][0:8, 512:1024], start=True, stop=True), "pe")
        em.waits("sc", [t_wr, st["wrep_sc"]])
        t_we = em.op("sc", lambda s: s.copy(wrep[:, :], pW2[:, 0:1024]), "sc")
        st["pw2"][0] = t_we
        st["pw2"][1] = t_we
        em.wait("pe", zb_ready[k])
        for fc in range(NFC):
            n1 = w1_consumed["n"]
            slot = n1 % 3
            prefetch_w1(n1 + 2)
            prefetch_w2(w2_consumed["n"] + 1)
            em.wait("pe", w1_gd[n1])
            for half in range(2):
                pp = (fc * 2 + half) % 2
                em.wait("pe", st["pw1"][pp])
                for hc in range(CCN):
                    t_mm = em.op("pe", lambda t, slot=slot, hc=hc, half=half,
                                 pp=pp, par=par:
                                 t.matmul(pW1[:, pp * 512:(pp + 1) * 512],
                                          w1r[:, slot * 1024 + hc * 128:
                                              slot * 1024 + (hc + 1) * 128],
                                          zb[par][:, hc * S + half * 512:
                                                  hc * S + half * 512 + 512],
                                          start=(hc == 0), stop=(hc == 7)),
                                 "pe" if hc == 7 else None)
                hreg = hid[:, fc * S + half * 512: fc * S + half * 512 + 512]
                em.waits("sc", [t_mm, st["hid_free_pe"]])
                t_g = em.op("sc", lambda s, pp=pp, hreg=hreg, fc=fc, e=e:
                            s.activation(
                    hreg, pW1[:, pp * 512:(pp + 1) * 512],
                    AF.Gelu, bias=b1col[:, fc * 8 + e: fc * 8 + e + 1]), "sc")
                st["pw1"][pp] = t_g
                em.waits("ve", [t_g, t_we])
                t_h = em.op("ve", lambda v, hreg=hreg, half=half: v.tensor_tensor(
                    hreg, hreg, wrep[:, half * 512:(half + 1) * 512],
                    ALU.mult), "ve")
                pump(pumpw, 1)
            w1_slab_tok.append(em.tok("pe"))
            w1_consumed["n"] += 1
        hid_ready = t_h
        st["wrep_sc"] = em.tok("ve")
        em.wait("pe", hid_ready)
        for mc in range(NMC):
            n2 = w2_consumed["n"]
            slot = n2 % 2
            prefetch_w2(n2 + 1)
            prefetch_w1(w1_consumed["n"] + 2)
            em.wait("pe", w2_gd[n2])
            for half in range(2):
                pp = (mc * 2 + half) % 2
                em.wait("pe", st["pw2"][pp])
                if e == 0:
                    em.op("pe", lambda t, mc=mc, half=half, pp=pp, par=par: t.matmul(
                        pW2[:, pp * 512:(pp + 1) * 512],
                        b2bf[0:8, mc * 128:(mc + 1) * 128],
                        wfm[par][0:8, half * 512:(half + 1) * 512],
                        start=True, stop=False))
                for fc in range(NFC):
                    t_mm = em.op("pe", lambda t, slot=slot, fc=fc, half=half, pp=pp,
                                 first=(e != 0 and fc == 0):
                                 t.matmul(pW2[:, pp * 512:(pp + 1) * 512],
                                          w2r[:, slot * 2048 + fc * 128:
                                              slot * 2048 + (fc + 1) * 128],
                                          hid[:, fc * S + half * 512:
                                              fc * S + half * 512 + 512],
                                          start=first, stop=(fc == NFC - 1)),
                                 "pe" if fc == NFC - 1 else None)
                em.wait("ve", t_mm)
                t_ev = em.op("ve", lambda v, mc=mc, half=half, pp=pp, par=par:
                             v.tensor_tensor(
                                 x1[par][:, mc * S + half * 512:
                                         mc * S + half * 512 + 512],
                                 x1[par][:, mc * S + half * 512:
                                         mc * S + half * 512 + 512],
                                 pW2[:, pp * 512:(pp + 1) * 512], ALU.add), "ve")
                st["pw2"][pp] = t_ev
                pump(pumpw, 1)
            w2_slab_tok.append(em.tok("pe"))
            w2_consumed["n"] += 1
        st["hid_free_pe"] = em.tok("pe")
        return t_ev

    otm_half = {0: None, 1: None}

    def emit_outphase(k, x1_done):
        par = k % 2
        for j in range(8):
            for half in range(2):
                em.waits("pe", [x1_done, st["pw2"][half]])
                for c4 in range(4):
                    cc = half * 4 + c4
                    t_tp = em.op("pe", lambda t, cc=cc, c4=c4, j=j, par=par,
                                 half=half: t.transpose(
                        pW2[:, half * 512 + c4 * 128: half * 512 + c4 * 128 + 128],
                        x1[par][:, cc * S + j * 128: cc * S + j * 128 + 128],
                        id32[:, :]), "pe" if c4 == 3 else None)
                em.waits("sc", [t_tp, otm_half[half]])
                t_e = em.op("sc", lambda s, half=half: s.copy(
                    otm[:, half * 512:(half + 1) * 512],
                    pW2[:, half * 512:(half + 1) * 512]), "sc")
                st["pw2"][half] = t_e
                em.wait("sy", t_e)
                t_od = em.op("sy", lambda sy, k=k, j=j, half=half: sy.dma_start(
                    y_d[(k * 8 + j) * 128: (k * 8 + j) * 128 + 128,
                        half * 512:(half + 1) * 512],
                    otm[:, half * 512:(half + 1) * 512]), "od", 16)
                otm_half[half] = t_od
        st["x1_free"][par] = em.tok("pe")

    tap("kcol", kcol[:, :])
    WQ[0] = build_wq(0)
    pump(0, 10 ** 6)
    emit_rz(0)
    tap("un0", unb[:, :])
    tap("rows3", rows3[0:3, 32:1184])
    tap("zbd", zb[0][:, :])
    tap("wfmd", wfm[0][:, :])
    tap("x1r", x1[0][:, :])
    prefetch_w1(2)
    prefetch_w2(1)

    for k in range(NSUP):
        if k + 1 < NSUP:
            WQ[k + 1] = build_wq(k + 1)
        for e in range(NE):
            t_last = emit_expert(k, e, k + 1 if k + 1 < NSUP else None)
            if k == 0 and e == 0:
                tap("hidd", hid[:, :])
        st["experts_pe_done"][k] = em.tok("pe")
        if k == 0:
            tap("x1e", x1[0][:, :])
        if k + 1 < NSUP:
            pump(k + 1, 10 ** 6)
            emit_rz(k + 1)
        emit_outphase(k, t_last)

    em.op("sy", lambda sy: sy.wait_ge(em.sems["od"], em.cnt["od"]))

    with nc.Block() as block:
        @block.gpsimd
        def _(g):
            for fn in em.q["gp"]:
                fn(g)

        @block.tensor
        def _(t):
            for fn in em.q["pe"]:
                fn(t)

        @block.vector
        def _(v):
            for fn in em.q["ve"]:
                fn(v)

        @block.scalar
        def _(s):
            for fn in em.q["sc"]:
                fn(s)

        @block.sync
        def _(sy):
            for fn in em.q["sy"]:
                fn(sy)

    ctx.close()
    return nc


def _host_inputs(core_u, A, B, C, Dv, log_dt, ln_g, ln_b, rW, rb, W1, b1, W2, b2):
    f32 = np.float32
    bf = ml_dtypes.bfloat16
    return {
        "u": np.ascontiguousarray(core_u, dtype=f32),
        "Acol": np.ascontiguousarray(np.reshape(A, (64, 1)), dtype=f32),
        "Bmat": np.ascontiguousarray(B, dtype=f32),
        "Ccol": np.ascontiguousarray(
            np.reshape(C, (8, 128, 64)).transpose(1, 0, 2).reshape(128, 512),
            dtype=f32),
        "Dcol": np.ascontiguousarray(np.reshape(Dv, (8, 128)).T, dtype=f32),
        "ldtrow": np.ascontiguousarray(np.reshape(log_dt, (1, 1024)), dtype=f32),
        "gcol": np.ascontiguousarray(np.reshape(ln_g, (8, 128)).T, dtype=f32),
        "bcol": np.ascontiguousarray(np.reshape(ln_b, (8, 128)).T, dtype=f32),
        "rWp": np.ascontiguousarray(
            np.reshape(rW, (8, 128, 8)).transpose(1, 0, 2).reshape(128, 64),
            dtype=f32),
        "rbrow": np.ascontiguousarray(np.reshape(rb, (1, 8)), dtype=f32),
        "W1": np.ascontiguousarray(W1, dtype=f32),
        "b1col": np.ascontiguousarray(
            np.reshape(b1, (8, 16, 128)).transpose(2, 1, 0).reshape(128, 128),
            dtype=f32),
        "W2": np.ascontiguousarray(W2, dtype=f32),
        "b2sb": np.ascontiguousarray(b2, dtype=f32),
        "ident32": np.eye(128, dtype=f32),
        "identbf": np.eye(128, dtype=f32).astype(bf),
        "ones32": np.ones((128, 1), dtype=f32),
        "onesbf": np.ones((128, 1), dtype=f32).astype(bf),
        "ones1b": np.ones((1, 128), dtype=f32).astype(bf),
        "oneh8": np.repeat(np.eye(8, dtype=f32), 128, axis=1).astype(bf),
        "oneh3": np.repeat(np.eye(3, dtype=f32), 128, axis=1).astype(bf),
        "cvals": np.tile(np.array([[1e-5, -0.5]], dtype=f32), (128, 1)),
    }


def kernel(L0_attn_u, L0_attn_A, L0_attn_B, L0_attn_C, L0_attn_D, L0_attn_log_dt,
           L0_ffn_ln_g, L0_ffn_ln_b, L0_ffn_rW, L0_ffn_rb, L0_ffn_W1, L0_ffn_b1,
           L0_ffn_W2, L0_ffn_b2):
    u = np.asarray(L0_attn_u)
    args = tuple(np.asarray(a) for a in (
        L0_attn_A, L0_attn_B, L0_attn_C, L0_attn_D, L0_attn_log_dt,
        L0_ffn_ln_g, L0_ffn_ln_b, L0_ffn_rW, L0_ffn_rb, L0_ffn_W1, L0_ffn_b1,
        L0_ffn_W2, L0_ffn_b2))
    nc = build_nc()
    in_maps = [_host_inputs(u[c], *args) for c in range(8)]
    res = run_bass_kernel_spmd(nc, in_maps, core_ids=list(range(8)))
    return np.stack([np.asarray(r["y"], dtype=np.float32) for r in res.results],
                    axis=0)
